# version 08
# baseline (speedup 1.0000x reference)
import numpy as np
from contextlib import ExitStack

import concourse.bass as bass
import concourse.mybir as mybir
from concourse.bass_utils import run_bass_kernel_spmd

F32 = mybir.dt.float32
BF16 = mybir.dt.bfloat16
AF = mybir.ActivationFunctionType
ALU = mybir.AluOpType

D = 2048
T = 2048
NDK = 16
NCH = 16
NH = 16
FH = 8192
NFH = 64
EPS_LN = 1e-5
EPS_AT = 1e-6
FWHT_SCALE = 1.0 / float(np.sqrt(FH))

ENGS = ("sync", "gpsimd", "tensor", "vector", "scalar")


class Prog:

    NQROT = 8

    def __init__(self, nc):
        self.nc = nc
        self.steps = {e: [] for e in ENGS}
        self.counts = {}
        self.sems = {}
        self.last_step = {e: None for e in ENGS}
        self.qrot = {}
        self.qnames = {}

    def sem(self, name):
        if name not in self.sems:
            self.sems[name] = self.nc.alloc_semaphore(name)
            self.counts[name] = 0
        return self.sems[name]

    def step(self, eng, emit, inc=None, inc_by=1, waits=(), is_dma=False):
        ev = None
        if inc is not None:
            self.sem(inc)
            self.counts[inc] += inc_by
            ev = (inc, self.counts[inc])
        s = {"emit": emit, "ev": ev, "inc_by": inc_by, "waits": [w for w in waits if w],
             "extra_inc": [], "is_dma": is_dma, "barred": False}
        self.steps[eng].append(s)
        self.last_step[eng] = s
        return ev

    def dma(self, eng, out, in_, q, waits=()):
        idx = self.qrot.get(q, 0)
        self.qrot[q] = idx + 1
        semname = f"{q}_{idx % self.NQROT}"
        if q not in self.qnames:
            self.qnames[q] = [f"{q}_{k}" for k in range(self.NQROT)]
            for sn in self.qnames[q]:
                self.sem(sn)

        def emit(h, out=out, in_=in_):
            return h.dma_start(out=out, in_=in_)
        return self.step(eng, emit, inc=semname, inc_by=16, waits=waits, is_dma=True)

    def queue_totals(self, q=None):
        qs = [q] if q else list(self.qnames)
        return [(sn, self.counts[sn]) for qq in qs for sn in self.qnames[qq]
                if self.counts[sn] > 0]

    def barrier(self):
        evs = []
        for e in ENGS:
            if self.last_step[e] is None:
                continue
            for st in reversed(self.steps[e]):
                if st["ev"] is not None:
                    evs.append(st["ev"])
                    break
            self.last_step[e] = None
        evs += self.queue_totals()
        evs = sorted(set(evs))
        for e in ENGS:
            self.step(e, None, waits=list(evs))

    def emit_all(self, block):
        for eng in ENGS:
            steps = self.steps[eng]
            if not steps:
                continue
            dec = getattr(block, eng)

            def section(h, steps=steps):
                for s in steps:
                    for (sn, v) in s["waits"]:
                        h.wait_ge(self.sems[sn], v)
                    ins = s["emit"](h) if s["emit"] is not None else None
                    if ins is not None:
                        if s["ev"] is not None:
                            ins.then_inc(self.sems[s["ev"][0]], s["inc_by"])
                        for (sn, by) in s["extra_inc"]:
                            ins.then_inc(self.sems[sn], by)
                    else:
                        assert s["ev"] is None, "eventful step with no instruction"
            dec(section)


def build(dump_scratch=False, upto="all", dump=()):
    nc = bass.Bass("TRN2", target_bir_lowering=False)
    def sk(name):
        if dump_scratch or name in dump:
            return "ExternalOutput"
        return "Internal"

    x_fm = nc.dram_tensor("x_fm", [D, T], F32, kind="ExternalInput").ap()
    wq = nc.dram_tensor("wq", [D, D], F32, kind="ExternalInput").ap()
    wk = nc.dram_tensor("wk", [D, D], F32, kind="ExternalInput").ap()
    wv = nc.dram_tensor("wv", [D, D], F32, kind="ExternalInput").ap()
    wo = nc.dram_tensor("wo", [D, D], F32, kind="ExternalInput").ap()
    w1t = nc.dram_tensor("w1t", [D, FH], F32, kind="ExternalInput").ap()
    w3t = nc.dram_tensor("w3t", [D, FH], F32, kind="ExternalInput").ap()
    w2t = nc.dram_tensor("w2t", [FH, D], F32, kind="ExternalInput").ap()
    bcols = nc.dram_tensor("bcols", [128, 48], F32, kind="ExternalInput").ap()
    bvrow = nc.dram_tensor("bvrow", [1, D], F32, kind="ExternalInput").ap()
    consts = nc.dram_tensor("consts", [128, 448], BF16, kind="ExternalInput").ap()
    constf = nc.dram_tensor("constf", [128, 128], F32, kind="ExternalInput").ap()
    mask8 = nc.dram_tensor("mask8", [128, 8], F32, kind="ExternalInput").ap()
    epsc = nc.dram_tensor("epsc", [1, 4], F32, kind="ExternalInput").ap()

    out_fm = nc.dram_tensor("out_fm", [D, T], F32, kind="ExternalOutput").ap()

    QS = nc.dram_tensor("QS", [NDK, 128, T], BF16, kind=sk("QS")).ap()
    KS = nc.dram_tensor("KS", [NDK, 128, T], BF16, kind=sk("KS")).ap()
    VS = nc.dram_tensor("VS", [NCH, 128, D], BF16, kind=sk("VS")).ap()
    KTS = nc.dram_tensor("KTS", [NCH, 128, D], BF16, kind=sk("KTS")).ap()
    YS = nc.dram_tensor("YS", [NDK, 128, T], F32, kind=sk("YS")).ap()
    HID = nc.dram_tensor("HID", [NFH, 128, T], BF16, kind=sk("HID")).ap()
    A2 = nc.dram_tensor("A2", [64, 128, D], BF16, kind=sk("A2")).ap()
    W2F = nc.dram_tensor("W2F", [NFH, 128, D], BF16, kind=sk("W2F")).ap()
    CCI = nc.dram_tensor("CCI", [128, 2064], F32, kind="Internal").ap()
    CCO = nc.dram_tensor("CCO", [8 * 128, 2064], F32, kind="Internal",
                         addr_space="Local").ap()
    XN_D = nc.dram_tensor("XN_D", [128, NDK * T], BF16, kind="ExternalOutput").ap() \
        if dump_scratch else None
    if dump_scratch:
        ATT4_D = nc.dram_tensor("ATT4_D", [128, NH * 512], BF16, kind="ExternalOutput").ap()
        NUM_D = nc.dram_tensor("NUM_D", [1, 128, D], F32, kind="ExternalOutput").ap()
        ZR_D = nc.dram_tensor("ZR_D", [1, 128, 16], F32, kind="ExternalOutput").ap()

    uid_c = [0]

    def uid():
        uid_c[0] += 1
        return uid_c[0]

    es = ExitStack()
    with es:
        sb = lambda name, shape, dt: es.enter_context(nc.sbuf_tensor(name, shape, dt))
        cst = sb("cst", [128, 448], BF16)
        maskf = sb("maskf", [128, 128], F32)
        ones_f32 = sb("ones_f32", [1, 128], F32)
        bcol_sb = sb("bcol_sb", [128, 48], F32)
        mask8_sb = sb("mask8_sb", [128, 8], F32)
        bv_bc = sb("bv_bc", [128, D], F32)
        eps_sb = sb("eps_sb", [1, 4], F32)
        state_kv = sb("state_kv", [128, NH * 128], F32)
        state_ks = sb("state_ks", [128, NH], F32)

        MASK = maskf[:]
        S128 = cst[:, 128:256]
        IDENT = cst[:, 256:384]
        S64 = cst[0:64, 384:448]
        S64H = cst[64:128, 384:448]
        ONES_COL = cst[:, 127:128]

        p = Prog(nc)
        for s_ in ("sPE", "sDVE", "sACT", "sPOOL", "sCC"):
            p.sem(s_)
        nc.const_aps.register(F32, EPS_LN, eps_sb[0:1, 0:1])

        with ExitStack() as pes:
            bvrow_sb = pes.enter_context(nc.sbuf_tensor("bvrow_sb", [1, D], F32))
            psBV = pes.enter_context(nc.psum_tensor("psBV", [128, D], F32))

            e1 = p.dma("sync", cst[:], consts, "qS")
            e2 = p.dma("sync", bcol_sb[:], bcols, "qS")
            e3 = p.dma("sync", mask8_sb[:], mask8, "qS")
            e4 = p.dma("sync", bvrow_sb[:], bvrow, "qS")
            e5 = p.dma("sync", maskf[:], constf, "qS")
            p.dma("sync", eps_sb[:], epsc, "qS")
            ev = p.step("vector", lambda v: v.memset(ones_f32[:], 1.0), inc="sDVE")

            def mk_bv(t):
                for tb in range(4):
                    mm = t.matmul(out=psBV[:, tb * 512:(tb + 1) * 512], lhsT=ones_f32[:],
                                  rhs=bvrow_sb[0:1, tb * 512:(tb + 1) * 512],
                                  start=True, stop=True)
                return mm
            epe = p.step("tensor", mk_bv, inc="sPE", waits=[e4, ev])
            p.step("scalar", lambda s: s.copy(out=bv_bc[:], in_=psBV[:]),
                   inc="sACT", waits=[epe])
            p.barrier()

        def ln_fm(src_getter, dst, chase=False):
            with ExitStack() as pes:
                xbf = pes.enter_context(nc.sbuf_tensor(f"ln_xbf{uid()}", [128, NDK * T], BF16))
                rows = pes.enter_context(nc.sbuf_tensor(f"ln_rows{uid()}", [1, 3 * T], F32))
                with ExitStack() as ses:
                    xt = [ses.enter_context(nc.sbuf_tensor(f"ln_xt{uid()}", [128, T], F32)) for i in range(2)]
                    sqbf = [ses.enter_context(nc.sbuf_tensor(f"ln_sq{uid()}", [128, T], BF16)) for i in range(2)]
                    psS = ses.enter_context(nc.psum_tensor(f"ln_psS{uid()}", [1, T], F32))
                    psQ = ses.enter_context(nc.psum_tensor(f"ln_psQ{uid()}", [1, T], F32))

                    ld, cast_e, sq_e, pe_e = [], [], [], []
                    for dk in range(NDK):
                        w = []
                        if dk >= 2:
                            w = [cast_e[dk - 2], sq_e[dk - 2]]
                        ld.append(p.dma("sync", xt[dk % 2][:], src_getter(dk), "qS", waits=w))
                        cast_e.append(p.step(
                            "vector",
                            (lambda v, dk=dk: v.tensor_copy(out=xbf[:, dk * T:(dk + 1) * T],
                                                            in_=xt[dk % 2][:])),
                            inc="sDVE", waits=[ld[dk]]))
                        w = [ld[dk]] + ([pe_e[dk - 2]] if dk >= 2 else [])
                        sq_e.append(p.step(
                            "scalar",
                            (lambda s, dk=dk: s.activation(out=sqbf[dk % 2][:], in_=xt[dk % 2][:],
                                                           func=AF.Square)),
                            inc="sACT", waits=w))

                        def mm_stats(t, dk=dk):
                            for tb in range(4):
                                t.matmul(out=psS[0:1, tb * 512:(tb + 1) * 512], lhsT=ONES_COL,
                                         rhs=xbf[:, dk * T + tb * 512:dk * T + (tb + 1) * 512],
                                         start=(dk == 0), stop=(dk == NDK - 1))
                            for tb in range(4):
                                mm = t.matmul(out=psQ[0:1, tb * 512:(tb + 1) * 512], lhsT=ONES_COL,
                                              rhs=sqbf[dk % 2][:, tb * 512:(tb + 1) * 512],
                                              start=(dk == 0), stop=(dk == NDK - 1))
                            return mm
                        pe_e.append(p.step("tensor", mm_stats, inc="sPE",
                                           waits=[cast_e[dk], sq_e[dk]]))

                    def ev_rows(s):
                        s.copy(out=rows[0:1, 0:T], in_=psS[:])
                        return s.copy(out=rows[0:1, T:2 * T], in_=psQ[:])
                    er = p.step("scalar", ev_rows, inc="sACT", waits=[pe_e[-1]])

                    def rowmath_v(v):
                        v.tensor_scalar_mul(out=rows[0:1, 0:T],
                                            in0=rows[0:1, 0:T], scalar1=1.0 / T)
                        v.tensor_tensor(out=rows[0:1, 2 * T:3 * T], in0=rows[0:1, 0:T],
                                        in1=rows[0:1, 0:T], op=ALU.mult)
                        return v.scalar_tensor_tensor(
                            out=rows[0:1, T:2 * T], in0=rows[0:1, T:2 * T], scalar=1.0 / T,
                            in1=rows[0:1, 2 * T:3 * T], op0=ALU.mult, op1=ALU.subtract)
                    ev1 = p.step("vector", rowmath_v, inc="sDVE", waits=[er])

                    def rowmath_s(s):
                        s.activation(out=rows[0:1, T:2 * T], in_=rows[0:1, T:2 * T],
                                     func=AF.Ln, bias=EPS_LN)
                        return s.activation(out=rows[0:1, 2 * T:3 * T], in_=rows[0:1, T:2 * T],
                                            func=AF.Exp, scale=-0.5)
                    ev2 = p.step("scalar", rowmath_s, inc="sACT", waits=[ev1])

                    ev3 = p.step("vector", lambda v: v.scalar_tensor_tensor(
                        out=rows[0:1, T:2 * T], in0=rows[0:1, 0:T], scalar=-1.0,
                        in1=rows[0:1, 2 * T:3 * T], op0=ALU.mult, op1=ALU.mult),
                        inc="sDVE", waits=[ev2])
                    p.barrier()

                with ExitStack() as bes:
                    psBa = bes.enter_context(nc.psum_tensor(f"ln_psBa{uid()}", [128, T], F32))
                    psBb = bes.enter_context(nc.psum_tensor(f"ln_psBb{uid()}", [128, T], F32))
                    a_bc = bes.enter_context(nc.sbuf_tensor(f"ln_abc{uid()}", [128, T], F32))
                    b_bc = bes.enter_context(nc.sbuf_tensor(f"ln_bbc{uid()}", [128, T], F32))
                    tmpD = bes.enter_context(nc.sbuf_tensor(f"ln_tmpD{uid()}", [128, T], F32))
                    tmpP = bes.enter_context(nc.sbuf_tensor(f"ln_tmpP{uid()}", [128, T], F32))

                    def mk_bc(t):
                        for tb in range(4):
                            t.matmul(out=psBa[:, tb * 512:(tb + 1) * 512], lhsT=ones_f32[:],
                                     rhs=rows[0:1, 2 * T + tb * 512:2 * T + (tb + 1) * 512],
                                     start=True, stop=True)
                            mm = t.matmul(out=psBb[:, tb * 512:(tb + 1) * 512], lhsT=ones_f32[:],
                                          rhs=rows[0:1, T + tb * 512:T + (tb + 1) * 512],
                                          start=True, stop=True)
                        return mm
                    ebc = p.step("tensor", mk_bc, inc="sPE")

                    def ev_bc(s):
                        s.copy(out=a_bc[:], in_=psBa[:])
                        return s.copy(out=b_bc[:], in_=psBb[:])
                    eev = p.step("scalar", ev_bc, inc="sACT", waits=[ebc])

                    nrm = []
                    for dk in range(NDK):
                        dve = (dk % 8 < 5) and not chase or chase
                        eng = "vector" if dve else "gpsimd"
                        tb_ = tmpD if dve else tmpP

                        def nm(h, dk=dk, tb_=tb_):
                            h.tensor_tensor(out=tb_[:], in0=xbf[:, dk * T:(dk + 1) * T],
                                            in1=a_bc[:], op=ALU.mult)
                            return h.tensor_tensor(out=dst[:, dk * T:(dk + 1) * T],
                                                   in0=tb_[:], in1=b_bc[:], op=ALU.add)
                        nrm.append(p.step(eng, nm, inc=("sDVE" if dve else "sPOOL"),
                                          waits=[eev]))
                    p.barrier()
                    return nrm

        xn_es = ExitStack()
        xn = xn_es.enter_context(nc.sbuf_tensor("xn1", [128, NDK * T], BF16))
        nrm1 = ln_fm(lambda dk: x_fm[dk * 128:(dk + 1) * 128, :], xn, chase=False)
        if dump_scratch:
            p.dma("sync", XN_D, xn[:], "qS")
            p.barrier()

        def load_w_half(w_ap, half, dstbuf, waits=()):
            src = w_ap[:, half * 1024:(half + 1) * 1024].rearrange("(k p) o -> p k o", p=128)
            dst = dstbuf[:].rearrange("p (k o) -> p k o", o=1024)
            return p.dma("gpsimd", dst, src, "qG", waits=waits)

        def proj_fm(w_ap, bias_idx, dst_scr, xnl):
            with ExitStack() as pes:
                psP = [pes.enter_context(nc.psum_tensor(f"pj_ps{uid()}", [128, T], F32)) for i in range(2)]
                e0 = [pes.enter_context(nc.sbuf_tensor(f"pj_e0{uid()}", [128, T], F32)) for i in range(2)]
                rr = [pes.enter_context(nc.sbuf_tensor(f"pj_r{uid()}", [128, T], F32)) for i in range(2)]
                ot = [pes.enter_context(nc.sbuf_tensor(f"pj_o{uid()}", [128, T], BF16)) for i in range(2)]

                wl = [load_w_half(w_ap, 0, wbufA), load_w_half(w_ap, 1, wbufB)]
                pe_ev, act_ev, dve_ev, st_ev = [], [], [], []
                for ds in range(16):
                    half = ds // 8
                    wb = wbufA if half == 0 else wbufB
                    w = [wl[half]]
                    if ds >= 2:
                        w.append(act_ev[ds - 2])

                    def mms(t, ds=ds, wb=wb):
                        for dk in range(NDK):
                            for tb in range(4):
                                mm = t.matmul(
                                    out=psP[ds % 2][:, tb * 512:(tb + 1) * 512],
                                    lhsT=wb[:, dk * 1024 + (ds % 8) * 128: dk * 1024 + (ds % 8) * 128 + 128],
                                    rhs=xnl[:, dk * T + tb * 512: dk * T + (tb + 1) * 512],
                                    start=(dk == 0), stop=(dk == NDK - 1))
                        return mm
                    pe_ev.append(p.step("tensor", mms, inc="sPE", waits=w))

                    w = [pe_ev[ds]]
                    if ds >= 2:
                        w.append(dve_ev[ds - 2])

                    def act(s, ds=ds):
                        bias = bcol_sb[:, bias_idx * 16 + ds: bias_idx * 16 + ds + 1]
                        s.activation(out=e0[ds % 2][:], in_=psP[ds % 2][:], func=AF.Exp, bias=bias)
                        return s.activation(out=rr[ds % 2][:], in_=psP[ds % 2][:], func=AF.Relu, bias=bias)
                    act_ev.append(p.step("scalar", act, inc="sACT", waits=w))

                    w = [act_ev[ds]]
                    if ds >= 2:
                        w.append(st_ev[ds - 2])
                    dve_ev.append(p.step("vector", (lambda v, ds=ds: v.scalar_tensor_tensor(
                        out=ot[ds % 2][:], in0=e0[ds % 2][:], scalar=1.0, in1=rr[ds % 2][:],
                        op0=ALU.min, op1=ALU.add)), inc="sDVE", waits=w))
                    st_ev.append(p.dma("sync", dst_scr[ds], ot[ds % 2][:], "qS",
                                       waits=[dve_ev[ds]]))
                p.barrier()

        def proj_tm_v(xnl, nrm):
            with ExitStack() as pes:
                psP = [pes.enter_context(nc.psum_tensor(f"pv_ps{i}", [128, D], F32)) for i in range(2)]
                ot = [pes.enter_context(nc.sbuf_tensor(f"pv_o{i}", [128, D], BF16)) for i in range(2)]
                wl = [load_w_half(wv, 0, wbufA), load_w_half(wv, 1, wbufB)]
                pe_ev, dve_ev, st_ev = [], [], []
                for ts_ in range(16):
                    w = [wl[0], wl[1]]
                    if ts_ >= 2:
                        w.append(dve_ev[ts_ - 2])

                    if ts_ == 0:
                        for dk in range(NDK):
                            def mmc(t, dk=dk):
                                for dq in range(4):
                                    wb = wbufA if dq < 2 else wbufB
                                    off = (dq % 2) * 512
                                    mm = t.matmul(
                                        out=psP[0][:, dq * 512:(dq + 1) * 512],
                                        lhsT=xnl[:, dk * T: dk * T + 128],
                                        rhs=wb[:, dk * 1024 + off: dk * 1024 + off + 512],
                                        start=(dk == 0), stop=(dk == NDK - 1))
                                return mm
                            ev = p.step("tensor", mmc, inc="sPE", waits=w + [nrm[dk]])
                            w = []
                        pe_ev.append(ev)
                    else:
                        def mms(t, ts_=ts_):
                            for dk in range(NDK):
                                for dq in range(4):
                                    wb = wbufA if dq < 2 else wbufB
                                    off = (dq % 2) * 512
                                    mm = t.matmul(
                                        out=psP[ts_ % 2][:, dq * 512:(dq + 1) * 512],
                                        lhsT=xnl[:, dk * T + ts_ * 128: dk * T + ts_ * 128 + 128],
                                        rhs=wb[:, dk * 1024 + off: dk * 1024 + off + 512],
                                        start=(dk == 0), stop=(dk == NDK - 1))
                            return mm
                        pe_ev.append(p.step("tensor", mms, inc="sPE", waits=w))
                    w = [pe_ev[ts_]]
                    if ts_ >= 2:
                        w.append(st_ev[ts_ - 2])
                    dve_ev.append(p.step("vector", (lambda v, ts_=ts_: v.tensor_tensor(
                        out=ot[ts_ % 2][:], in0=psP[ts_ % 2][:], in1=bv_bc[:], op=ALU.add)),
                        inc="sDVE", waits=w))
                    st_ev.append(p.dma("sync", VS[ts_], ot[ts_ % 2][:], "qS", waits=[dve_ev[ts_]]))
                p.barrier()

        wbuf_es = ExitStack()
        wbufA = wbuf_es.enter_context(nc.sbuf_tensor("wbufA", [128, NDK * 1024], BF16, side="right"))
        wbufB = wbuf_es.enter_context(nc.sbuf_tensor("wbufB", [128, NDK * 1024], BF16, side="right"))
        proj_tm_v(xn, nrm1)
        proj_fm(wk, 1, KS, xn)
        proj_fm(wq, 0, QS, xn)
        xn_es.close()
        if upto == "p2":
            wbuf_es.close()
            p.emit_all(es.enter_context(nc.Block()))
            build._last_prog = p
            return nc

        with ExitStack() as pes:
            kfn3 = [pes.enter_context(nc.sbuf_tensor(f"p3kf{i}", [128, D], BF16)) for i in range(2)]
            vcn3 = [pes.enter_context(nc.sbuf_tensor(f"p3vc{i}", [128, D], BF16)) for i in range(2)]
            ktm = [pes.enter_context(nc.sbuf_tensor(f"p3_kt{i}", [128, D], BF16)) for i in range(2)]
            t_sb = pes.enter_context(nc.sbuf_tensor("p3_T", [128, 2064], F32))
            psT = pes.enter_context(nc.psum_tensor("p3_psT", [128, 1024], BF16))
            psSU = [pes.enter_context(nc.psum_tensor(f"p3_su{i}", [128, 512], F32)) for i in range(2)]
            psKS = pes.enter_context(nc.psum_tensor("p3_ks", [128, 16], F32))

            zero_ev = p.step("vector", lambda v: (v.memset(state_kv[:], 0.0),
                                                  v.memset(state_ks[:], 0.0))[-1], inc="sDVE")
            ld_k, ld_v, tr_ev, ev_ev, su_ev, add_ev, ks_ev = [], [], [], [], [], [], []
            chunk_pe_last = []
            for n in range(NCH):
                w = [chunk_pe_last[n - 2]] if n >= 2 else []
                src = KS.rearrange("s p t -> p s t")[:, :, n * 128:(n + 1) * 128]
                ld_k.append(p.dma("sync", kfn3[n % 2][:].rearrange("p (s t) -> p s t", t=128),
                                  src, "qS", waits=w))
                ld_v.append(p.dma("scalar", vcn3[n % 2][:], VS[n], "qA", waits=w))
                for hg in range(4):
                    gidx = n * 4 + hg
                    w = [ld_k[n]]
                    if gidx >= 1:
                        w.append(ev_ev[gidx - 1])

                    def trs(t, n=n, hg=hg):
                        for hh in range(4):
                            h = hg * 4 + hh
                            mm = t.transpose(
                                out=psT[:, (hg % 2) * 512 + hh * 128:(hg % 2) * 512 + hh * 128 + 128],
                                in_=kfn3[n % 2][:, h * 128:(h + 1) * 128], identity=IDENT)
                        return mm
                    tr_ev.append(p.step("tensor", trs, inc="sPE", waits=w))
                    ev_ev.append(p.step("scalar", (lambda s, n=n, hg=hg: s.activation(
                        out=ktm[n % 2][:, hg * 512:(hg + 1) * 512],
                        in_=psT[:, (hg % 2) * 512:(hg % 2) * 512 + 512], func=AF.Copy)),
                        inc="sACT", waits=[tr_ev[gidx]]))
                    w = [ev_ev[gidx], ld_v[n]]
                    if gidx >= 2:
                        w.append(add_ev[gidx - 2])
                    if hg == 0 and n >= 1:
                        w.append(ks_ev[n - 1])

                    def sus(t, n=n, hg=hg):
                        for hh in range(4):
                            h = hg * 4 + hh
                            t.matmul(out=psSU[hg % 2][:, hh * 128:(hh + 1) * 128],
                                     lhsT=ktm[n % 2][:, h * 128:(h + 1) * 128],
                                     rhs=vcn3[n % 2][:, h * 128:(h + 1) * 128],
                                     start=True, stop=True)
                        for hh in range(4):
                            h = hg * 4 + hh
                            mm = t.matmul(out=psKS[:, h:h + 1],
                                          lhsT=ktm[n % 2][:, h * 128:(h + 1) * 128],
                                          rhs=ONES_COL, start=True, stop=True)
                        return mm
                    su_ev.append(p.step("tensor", sus, inc="sPE", waits=w))
                    add_ev.append(p.step("vector", (lambda v, hg=hg: v.tensor_tensor(
                        out=state_kv[:, hg * 512:(hg + 1) * 512],
                        in0=state_kv[:, hg * 512:(hg + 1) * 512],
                        in1=psSU[hg % 2][:], op=ALU.add)),
                        inc="sDVE", waits=[su_ev[gidx]] + ([zero_ev] if gidx == 0 else [])))
                ks_ev.append(p.step("vector", lambda v: v.tensor_tensor(
                    out=state_ks[:], in0=state_ks[:], in1=psKS[:], op=ALU.add),
                    inc="sDVE", waits=[su_ev[n * 4 + 3]]))
                chunk_pe_last.append(su_ev[n * 4 + 3])
                p.dma("sync", KTS[n], ktm[n % 2][:], "qS", waits=[ev_ev[n * 4 + 3]])

            cpy = p.step("vector", lambda v: (v.tensor_copy(out=t_sb[:, 0:2048], in_=state_kv[:]),
                                              v.tensor_copy(out=t_sb[:, 2048:2064], in_=state_ks[:]))[-1],
                         inc="sDVE", waits=[add_ev[-1], ks_ev[-1]])
            wo_l = [load_w_half(wo, 0, wbufA), load_w_half(wo, 1, wbufB)]

            w2a = pes.enter_context(nc.sbuf_tensor("w2a", [64, D], BF16))
            w2a2 = pes.enter_context(nc.sbuf_tensor("w2a2", [64, D], BF16))
            w2o = pes.enter_context(nc.sbuf_tensor("w2o", [64, D], BF16))
            w2o2 = pes.enter_context(nc.sbuf_tensor("w2o2", [64, D], BF16))
            psA_ = pes.enter_context(nc.psum_tensor("w2psA", [64, D], F32))
            wa_ld, wa_pe, wa_ev, wa_st = [], [], [], []
            w2t_j = w2t.rearrange("(i j) d -> j i d", j=128)
            for j2 in range(128):
                w = [wa_pe[j2 - 2]] if j2 >= 2 else []
                wa_ld.append(p.dma("gpsimd", (w2a if j2 % 2 == 0 else w2a2)[:],
                                   w2t_j[j2], "qG", waits=w))
                w = [wa_ld[j2]] + ([wa_ev[j2 - 1]] if j2 >= 1 else [])

                def mm_a(t, j2=j2):
                    for tb in range(4):
                        mm = t.matmul(out=psA_[:, tb * 512:(tb + 1) * 512], lhsT=S64,
                                      rhs=(w2a if j2 % 2 == 0 else w2a2)[:, tb * 512:(tb + 1) * 512],
                                      start=True, stop=True)
                    return mm
                wa_pe.append(p.step("tensor", mm_a, inc="sPE", waits=w))
                w = [wa_pe[j2]] + ([wa_st[j2 - 2]] if j2 >= 2 else [])
                wa_ev.append(p.step("scalar", (lambda s, j2=j2: s.activation(
                    out=(w2o if j2 % 2 == 0 else w2o2)[:], in_=psA_[:], func=AF.Copy)),
                    inc="sACT", waits=w))
                wa_st.append(p.dma("sync", A2.rearrange("i j d -> j i d")[j2],
                                   (w2o if j2 % 2 == 0 else w2o2)[:], "qS",
                                   waits=[wa_ev[j2]]))
            eci = p.dma("gpsimd", CCI, t_sb[:], "qG", waits=[cpy])
            p.step("gpsimd", lambda g: g.collective_compute(
                "AllGather", ALU.bypass, ins=[CCI.opt()], outs=[CCO.opt()],
                replica_groups=[list(range(8))]), inc="sCC", waits=[eci])
            p.last_step["gpsimd"] = None
            p.barrier()

        if upto == "p3":
            wbuf_es.close()
            p.emit_all(es.enter_context(nc.Block()))
            build._last_prog = p
            return nc

        with ExitStack() as pes:
            a2i = [pes.enter_context(nc.sbuf_tensor(f"a2i{i}", [128, 2 * D], BF16)) for i in range(2)]
            w2ob = [pes.enter_context(nc.sbuf_tensor(f"w2fo{i}", [128, D], BF16)) for i in range(2)]
            psB_ = [pes.enter_context(nc.psum_tensor(f"w2psB{i}", [128, D], F32)) for i in range(2)]
            ld, pe_ev, act_ev, st_ev = [], [], [], []
            for i in range(64):
                ip, ii = i // 2, i % 2
                if ii == 0:
                    w = [pe_ev[i - 3]] if i >= 3 else []
                    srcap = A2.rearrange("i p d -> p i d")[:, 2 * ip:2 * ip + 2, :]
                    ld.append(p.dma("sync", a2i[ip % 2][:].rearrange("p (i d) -> p i d", d=D),
                                    srcap, "qS", waits=w))
                w = [ld[ip]] + ([act_ev[i - 2]] if i >= 2 else [])

                def mm_b(t, ip=ip, ii=ii, i=i):
                    for tb in range(4):
                        mm = t.matmul(out=psB_[i % 2][:, tb * 512:(tb + 1) * 512],
                                      lhsT=S128,
                                      rhs=a2i[ip % 2][:, ii * D + tb * 512: ii * D + (tb + 1) * 512],
                                      start=True, stop=True)
                    return mm
                pe_ev.append(p.step("tensor", mm_b, inc="sPE", waits=w))
                w = [pe_ev[i]] + ([st_ev[i - 2]] if i >= 2 else [])
                act_ev.append(p.step("scalar", (lambda s, i=i: s.activation(
                    out=w2ob[i % 2][:], in_=psB_[i % 2][:], func=AF.Copy,
                    scale=FWHT_SCALE)), inc="sACT", waits=w))
                st_ev.append(p.dma("scalar", W2F[i], w2ob[i % 2][:], "qA",
                                   waits=[act_ev[i]]))
            p.barrier()

        with ExitStack() as pes:
            pf = [pes.enter_context(nc.sbuf_tensor(f"pf{i}", [128, 2064], F32)) for i in range(2)]
            z_ev = p.step("vector", lambda v: (v.memset(state_kv[:], 0.0),
                                               v.memset(state_ks[:], 0.0))[-1], inc="sDVE")
            ld, ad = [], []
            for r in range(8):
                w = [("sCC", p.counts["sCC"])] if r == 0 else []
                if r >= 2:
                    w.append(ad[r - 2])
                ld.append(p.dma("sync", pf[r % 2][:], CCO[r * 128:(r + 1) * 128, :], "qS",
                                waits=w))
                def addpf(v, r=r):
                    v.scalar_tensor_tensor(out=state_kv[:], in0=pf[r % 2][:, 0:2048],
                                           scalar=mask8_sb[:, r:r + 1], in1=state_kv[:],
                                           op0=ALU.mult, op1=ALU.add)
                    return v.scalar_tensor_tensor(out=state_ks[:], in0=pf[r % 2][:, 2048:2064],
                                                  scalar=mask8_sb[:, r:r + 1], in1=state_ks[:],
                                                  op0=ALU.mult, op1=ALU.add)
                ad.append(p.step("vector", addpf, inc="sDVE",
                                 waits=[ld[r]] + ([z_ev] if r == 0 else [])))
            p.barrier()

        if upto == "pw2":
            wbuf_es.close()
            p.emit_all(es.enter_context(nc.Block()))
            build._last_prog = p
            return nc

        with ExitStack() as pes:
            qfn = [pes.enter_context(nc.sbuf_tensor(f"p4_qf{i}", [128, D], BF16)) for i in range(2)]
            kfn = [pes.enter_context(nc.sbuf_tensor(f"p4_kf{i}", [128, D], BF16)) for i in range(2)]
            ktn = [pes.enter_context(nc.sbuf_tensor(f"p4_kt{i}", [128, D], BF16)) for i in range(2)]
            vcn = [pes.enter_context(nc.sbuf_tensor(f"p4_vc{i}", [128, D], BF16)) for i in range(2)]
            sbf_kv = pes.enter_context(nc.sbuf_tensor("p4_skv", [128, D], BF16))
            sbf_ks = pes.enter_context(nc.sbuf_tensor("p4_sks", [128, 16], BF16))
            atm = pes.enter_context(nc.sbuf_tensor("p4_atm", [128, D], BF16))
            num_sb = pes.enter_context(nc.sbuf_tensor("p4_num", [128, D], F32))
            ztmp = pes.enter_context(nc.sbuf_tensor("p4_z", [128, 16], F32))
            zr = pes.enter_context(nc.sbuf_tensor("p4_zr", [128, 16], F32))
            att_tm = pes.enter_context(nc.sbuf_tensor("p4_att", [128, D], BF16))
            att_fm16 = pes.enter_context(nc.sbuf_tensor("p4_af16", [128, NH * T], BF16))
            xt4 = [pes.enter_context(nc.sbuf_tensor(f"p4_xt{i}", [128, 512], F32)) for i in range(2)]
            y_sb = [pes.enter_context(nc.sbuf_tensor(f"p4_y{i}", [128, 512], F32)) for i in range(2)]

            B = [pes.enter_context(nc.psum_tensor(f"p4_B{i}", [128, 512], F32)) for i in range(6)]
            psZK = pes.enter_context(nc.psum_tensor("p4_zk", [128, 48], F32))
            psTT = pes.enter_context(nc.psum_tensor("p4_tt", [128, 1024], BF16))
            maskbc = bass.AP(maskf[:].tensor, 0, [[128, 128], [0, 4], [1, 128]])
            zrbc = bass.AP(zr[:].tensor, 0, [[16, 128], [1, 16], [0, 128]])

            chunk_done = []
            at_ev, mask_ev, su_ev, num_ev, z_ev = {}, {}, {}, {}, {}
            stadd_ev, nev_ev, trg_ev, tev_ev, oproj_ev, stt_ev, yst_ev = {}, {}, {}, {}, {}, {}, {}
            cast_ev, zeps_ev, zr_ev, ksadd_ev, div_ev = [], [], [], [], []

            for n in range(NCH):
                g4 = n // 4
                w_buf = [chunk_done[n - 2]] if n >= 2 else []
                lq = p.dma("sync", qfn[n % 2][:].rearrange("p (s t) -> p s t", t=128),
                           QS.rearrange("s p t -> p s t")[:, :, n * 128:(n + 1) * 128],
                           "qS", waits=w_buf)
                lk = p.dma("sync", kfn[n % 2][:].rearrange("p (s t) -> p s t", t=128),
                           KS.rearrange("s p t -> p s t")[:, :, n * 128:(n + 1) * 128],
                           "qS", waits=w_buf)
                lt = p.dma("scalar", ktn[n % 2][:], KTS[n], "qA", waits=w_buf)
                lv = p.dma("scalar", vcn[n % 2][:], VS[n], "qA", waits=w_buf)

                w = ([stadd_ev[(n - 1, g_)] for g_ in range(4)] + [ksadd_ev[n - 1]]) if n >= 1 else []
                cast_ev.append(p.step("scalar", lambda s: (
                    s.activation(out=sbf_kv[:], in_=state_kv[:], func=AF.Copy),
                    s.activation(out=sbf_ks[:], in_=state_ks[:], func=AF.Copy))[-1],
                    inc="sACT", waits=w))

                for g in range(4):
                    w = [lq, lk]
                    if n >= 1:
                        w.append(mask_ev[(n - 1, g)])

                    def ats(t, n=n, g=g):
                        for j in range(4):
                            h = g * 4 + j
                            mm = t.matmul(out=B[g][:, j * 128:(j + 1) * 128],
                                          lhsT=kfn[n % 2][:, h * 128:(h + 1) * 128],
                                          rhs=qfn[n % 2][:, h * 128:(h + 1) * 128],
                                          start=True, stop=True)
                        return mm
                    at_ev[(n, g)] = p.step("tensor", ats, inc="sPE", waits=w)

                for g in range(4):
                    w = [at_ev[(n, g)]]
                    if n >= 1:
                        w.append(z_ev[(n - 1, g)])

                    def msk(v, g=g):
                        return v.tensor_tensor(
                            out=atm[:, g * 512:(g + 1) * 512].rearrange("p (j e) -> p j e", e=128),
                            in0=B[g][:].rearrange("p (j e) -> p j e", e=128),
                            in1=maskbc, op=ALU.mult)
                    mask_ev[(n, g)] = p.step("vector", msk, inc="sDVE", waits=w)

                for g in range(4):
                    w = [lt, lv]
                    w.append(stadd_ev[(n, g - 1)] if g >= 1 else
                             (stadd_ev[(n - 1, 3)] if n >= 1 else None))

                    def sus(t, n=n, g=g):
                        for j in range(4):
                            h = g * 4 + j
                            mm = t.matmul(out=B[4][:, j * 128:(j + 1) * 128],
                                          lhsT=ktn[n % 2][:, h * 128:(h + 1) * 128],
                                          rhs=vcn[n % 2][:, h * 128:(h + 1) * 128],
                                          start=True, stop=True)
                        return mm
                    su_ev[(n, g)] = p.step("tensor", sus, inc="sPE",
                                           waits=[x for x in w if x])
                    stadd_ev[(n, g)] = p.step("vector", (lambda v, g=g: v.tensor_tensor(
                        out=state_kv[:, g * 512:(g + 1) * 512],
                        in0=state_kv[:, g * 512:(g + 1) * 512],
                        in1=B[4][:], op=ALU.add)),
                        inc="sDVE", waits=[su_ev[(n, g)], cast_ev[n]])

                    w = [cast_ev[n], lv, mask_ev[(n, g)]]
                    w.append(nev_ev[(n, g - 1)] if g >= 1 else
                             (nev_ev[(n - 1, 3)] if n >= 1 else None))

                    def nums(t, n=n, g=g):
                        for j in range(4):
                            h = g * 4 + j
                            t.matmul(out=B[5][:, j * 128:(j + 1) * 128],
                                     lhsT=qfn[n % 2][:, h * 128:(h + 1) * 128],
                                     rhs=sbf_kv[:, h * 128:(h + 1) * 128],
                                     start=True, stop=False)
                            mm = t.matmul(out=B[5][:, j * 128:(j + 1) * 128],
                                          lhsT=atm[:, h * 128:(h + 1) * 128],
                                          rhs=vcn[n % 2][:, h * 128:(h + 1) * 128],
                                          start=False, stop=True)
                        return mm
                    num_ev[(n, g)] = p.step("tensor", nums, inc="sPE",
                                            waits=[x for x in w if x])

                    w = [cast_ev[n], lt, mask_ev[(n, g)]]
                    if g == 0 and n >= 1:
                        w += [zeps_ev[n - 1], ksadd_ev[n - 1]]

                    def zs(t, n=n, g=g):
                        for j in range(4):
                            h = g * 4 + j
                            t.matmul(out=psZK[:, h:h + 1],
                                     lhsT=qfn[n % 2][:, h * 128:(h + 1) * 128],
                                     rhs=sbf_ks[:, h:h + 1], start=True, stop=False)
                            t.matmul(out=psZK[:, h:h + 1],
                                     lhsT=atm[:, h * 128:(h + 1) * 128],
                                     rhs=ONES_COL, start=False, stop=True)
                            mm = t.matmul(out=psZK[:, 16 + h:16 + h + 1],
                                          lhsT=ktn[n % 2][:, h * 128:(h + 1) * 128],
                                          rhs=ONES_COL, start=True, stop=True)
                        return mm
                    z_ev[(n, g)] = p.step("tensor", zs, inc="sPE",
                                          waits=[x for x in w if x])

                    w = [num_ev[(n, g)]]
                    if n >= 1:
                        w.append(div_ev[n - 1])
                    nev_ev[(n, g)] = p.step("scalar", (lambda s, g=g: s.activation(
                        out=num_sb[:, g * 512:(g + 1) * 512], in_=B[5][:], func=AF.Copy)),
                        inc="sACT", waits=w)

                zeps_ev.append(p.step("vector", lambda v: v.tensor_scalar_add(
                    out=ztmp[:], in0=psZK[:, 0:16], scalar1=EPS_AT),
                    inc="sDVE", waits=[z_ev[(n, 3)]]))
                zr_ev.append(p.step("vector", lambda v: v.reciprocal(out=zr[:], in_=ztmp[:]),
                                    inc="sDVE", waits=[zeps_ev[n]]))
                ksadd_ev.append(p.step("vector", lambda v: v.tensor_tensor(
                    out=state_ks[:], in0=state_ks[:], in1=psZK[:, 16:32], op=ALU.add),
                    inc="sDVE", waits=[z_ev[(n, 3)], cast_ev[n]]))
                w = [zr_ev[n]] + [nev_ev[(n, g_)] for g_ in range(4)]
                if n >= 1:
                    w.append(trg_ev[(n - 1, 3)])

                def divs(v):
                    return v.tensor_tensor(
                        out=att_tm[:].rearrange("p (h e) -> p h e", e=128),
                        in0=num_sb[:].rearrange("p (h e) -> p h e", e=128),
                        in1=zrbc, op=ALU.mult)
                div_ev.append(p.step("vector", divs, inc="sDVE", waits=w))

                for g in range(4):
                    w = [div_ev[n]]
                    gidx = n * 4 + g
                    if gidx >= 1:
                        w.append(tev_ev[(n, g - 1)] if g >= 1 else tev_ev[(n - 1, 3)])

                    def trp(t, n=n, g=g):
                        for j in range(4):
                            h = g * 4 + j
                            mm = t.transpose(
                                out=psTT[:, (g % 2) * 512 + j * 128:(g % 2) * 512 + j * 128 + 128],
                                in_=att_tm[:, h * 128:(h + 1) * 128], identity=IDENT)
                        return mm
                    trg_ev[(n, g)] = p.step("tensor", trp, inc="sPE", waits=w)

                    def tev(s, n=n, g=g):
                        for j in range(4):
                            h = g * 4 + j
                            ins = s.activation(
                                out=att_fm16[:, h * T + n * 128: h * T + n * 128 + 128],
                                in_=psTT[:, (g % 2) * 512 + j * 128:(g % 2) * 512 + j * 128 + 128],
                                func=AF.Copy)
                        return ins
                    tev_ev[(n, g)] = p.step("scalar", tev, inc="sACT", waits=[trg_ev[(n, g)]])
                chunk_done.append(trg_ev[(n, 3)])

            sweep_w0 = [tev_ev[(NCH - 1, g_)] for g_ in range(4)]
            stt_fl, yst_fl = [], []
            for m in range(64):
                g4, ds = m // 16, m % 16
                w = [stt_fl[m - 2]] if m >= 2 else []
                lx = p.dma("sync", xt4[m % 2][:],
                           x_fm[ds * 128:(ds + 1) * 128, g4 * 512:(g4 + 1) * 512],
                           "qS", waits=w)
                half = ds // 8
                wb = wbufA if half == 0 else wbufB
                w = list(sweep_w0) if m < 4 else [stt_fl[m - 4]]
                w.append(wo_l[half])

                def opj(t, g4=g4, ds=ds, m=m, wb=wb):
                    for ek in range(NH):
                        mm = t.matmul(
                            out=B[m % 4][:],
                            lhsT=wb[:, ek * 1024 + (ds % 8) * 128: ek * 1024 + (ds % 8) * 128 + 128],
                            rhs=att_fm16[:, ek * T + g4 * 512:(ek * T) + (g4 + 1) * 512],
                            start=(ek == 0), stop=(ek == NH - 1))
                    return mm
                oev = p.step("tensor", opj, inc="sPE", waits=w)
                w = [oev, lx]
                if m >= 2:
                    w.append(yst_fl[m - 2])
                stt_fl.append(p.step("vector", (lambda v, ds=ds, m=m: v.scalar_tensor_tensor(
                    out=y_sb[m % 2][:], in0=B[m % 4][:],
                    scalar=bcol_sb[:, 32 + ds:32 + ds + 1], in1=xt4[m % 2][:],
                    op0=ALU.add, op1=ALU.add)),
                    inc="sDVE", waits=w))
                yst_fl.append(p.dma("sync", YS[ds][:, g4 * 512:(g4 + 1) * 512],
                                    y_sb[m % 2][:], "qS", waits=[stt_fl[m]]))
            p.barrier()

        if upto == "p4":
            wbuf_es.close()
            p.emit_all(es.enter_context(nc.Block()))
            build._last_prog = p
            return nc

        wbuf_es.close()
        xn = es.enter_context(nc.sbuf_tensor("xn2", [128, NDK * T], BF16))
        nrm2 = ln_fm(lambda dk: YS[dk], xn, chase=False)

        with ExitStack() as pes:
            w1b = [pes.enter_context(nc.sbuf_tensor(f"p6w1{i}", [128, NDK * 128], BF16)) for i in range(2)]
            w3b = [pes.enter_context(nc.sbuf_tensor(f"p6w3{i}", [128, NDK * 128], BF16)) for i in range(2)]
            sg = pes.enter_context(nc.sbuf_tensor("p6_sg", [128, T], F32))
            hid = [pes.enter_context(nc.sbuf_tensor(f"p6_h{i}", [128, T], BF16)) for i in range(2)]
            psUg = pes.enter_context(nc.psum_tensor("p6_psA", [128, T], F32))
            psUl = pes.enter_context(nc.psum_tensor("p6_psB", [128, T], F32))

            l1, l3, g_ev, l_ev, si_ev, tt_ev, st_ev = [], [], [], [], [], [], []
            for hs in range(NFH):
                w = [g_ev[hs - 2], l_ev[hs - 2]] if hs >= 2 else []
                srcap = w1t[:, hs * 128:(hs + 1) * 128].rearrange("(k p) h -> p k h", p=128)
                l1.append(p.dma("gpsimd", w1b[hs % 2][:].rearrange("p (k h) -> p k h", h=128),
                                srcap, "qG", waits=w))
                srcap = w3t[:, hs * 128:(hs + 1) * 128].rearrange("(k p) h -> p k h", p=128)
                l3.append(p.dma("gpsimd", w3b[hs % 2][:].rearrange("p (k h) -> p k h", h=128),
                                srcap, "qG", waits=w))
                w = [l1[hs]] + ([si_ev[hs - 1]] if hs >= 1 else [])

                def mm_g(t, hs=hs, xnl=xn):
                    for dk in range(NDK):
                        for tb in range(4):
                            mm = t.matmul(out=psUg[:, tb * 512:(tb + 1) * 512],
                                          lhsT=w1b[hs % 2][:, dk * 128:(dk + 1) * 128],
                                          rhs=xnl[:, dk * T + tb * 512: dk * T + (tb + 1) * 512],
                                          start=(dk == 0), stop=(dk == NDK - 1))
                    return mm
                g_ev.append(p.step("tensor", mm_g, inc="sPE", waits=w))
                w = [l3[hs]] + ([tt_ev[hs - 1]] if hs >= 1 else [])

                def mm_l(t, hs=hs, xnl=xn):
                    for dk in range(NDK):
                        for tb in range(4):
                            mm = t.matmul(out=psUl[:, tb * 512:(tb + 1) * 512],
                                          lhsT=w3b[hs % 2][:, dk * 128:(dk + 1) * 128],
                                          rhs=xnl[:, dk * T + tb * 512: dk * T + (tb + 1) * 512],
                                          start=(dk == 0), stop=(dk == NDK - 1))
                    return mm
                l_ev.append(p.step("tensor", mm_l, inc="sPE", waits=w))
                si_ev.append(p.step("scalar", (lambda s, hs=hs: s.activation(
                    out=sg[:], in_=psUg[:], func=AF.Silu)), inc="sACT",
                    waits=[g_ev[hs]] + ([tt_ev[hs - 1]] if hs >= 1 else [])))
                w = [l_ev[hs], si_ev[hs]] + ([st_ev[hs - 2]] if hs >= 2 else [])
                tt_ev.append(p.step("vector", (lambda v, hs=hs: v.tensor_tensor(
                    out=hid[hs % 2][:], in0=sg[:], in1=psUl[:], op=ALU.mult)),
                    inc="sDVE", waits=w))
                st_ev.append(p.dma("sync", HID[hs], hid[hs % 2][:], "qS", waits=[tt_ev[hs]]))
            p.barrier()

        if upto == "p6a":
            p.emit_all(es.enter_context(nc.Block()))
            build._last_prog = p
            return nc

        with ExitStack() as pes:
            w2fb = [pes.enter_context(nc.sbuf_tensor(f"p6d_w{i}", [128, 8 * 512], BF16)) for i in range(3)]
            hid8 = [pes.enter_context(nc.sbuf_tensor(f"p6d_h{i}", [128, 8 * 1024], BF16)) for i in range(3)]
            yt = [pes.enter_context(nc.sbuf_tensor(f"p6d_y{i}", [128, 1024], F32)) for i in range(4)]
            ot = [pes.enter_context(nc.sbuf_tensor(f"p6d_o{i}", [128, 1024], F32)) for i in range(8)]
            psD = [pes.enter_context(nc.psum_tensor(f"p6d_ps{i}", [128, 1024], F32)) for i in range(4)]

            hload, wload, yload, blk_ev, res_ev, oste = {}, {}, {}, {}, {}, {}
            for it in range(8):
                half, grp = it // 4, it % 4
                for hb in range(8):
                    gi = it * 8 + hb
                    w = []
                    if gi >= 3:
                        prev = (it, hb - 3) if hb >= 3 else (it - 1, hb + 5)
                        w.append(blk_ev[prev])
                    eng = "sync" if hb % 2 == 0 else "scalar"
                    q = "qS" if hb % 2 == 0 else "qA"
                    hload[(it, hb)] = p.dma(
                        eng, hid8[gi % 3][:].rearrange("p (c t) -> p c t", t=1024),
                        HID.rearrange("c p t -> p c t")[:, 8 * hb:8 * hb + 8,
                                                        half * 1024:(half + 1) * 1024],
                        q, waits=w)
                    wload[(it, hb)] = p.dma(
                        "gpsimd", w2fb[gi % 3][:].rearrange("p (c e) -> p c e", e=512),
                        W2F.rearrange("c p d -> p c d")[:, 8 * hb:8 * hb + 8,
                                                        grp * 512:(grp + 1) * 512],
                        "qG", waits=list(w))
                    if hb == 0:
                        for sp in range(4):
                            ds = grp * 4 + sp
                            w = [res_ev[(it - 1, sp)]] if it >= 1 else []
                            yload[(it, sp)] = p.dma("gpsimd", yt[sp][:],
                                                    YS[ds][:, half * 1024:(half + 1) * 1024],
                                                    "qG", waits=w)
                    w = [wload[(it, hb)], hload[(it, hb)]]
                    if hb == 0 and it >= 1:
                        w += [res_ev[(it - 1, s_)] for s_ in range(4)]

                    def blk(t, it=it, hb=hb):
                        gi = it * 8 + hb
                        for hh in range(8):
                            hk = hb * 8 + hh
                            for sp in range(4):
                                for tb in range(2):
                                    mm = t.matmul(
                                        out=psD[sp][:, tb * 512:(tb + 1) * 512],
                                        lhsT=w2fb[gi % 3][:, hh * 512 + sp * 128: hh * 512 + sp * 128 + 128],
                                        rhs=hid8[gi % 3][:, hh * 1024 + tb * 512: hh * 1024 + (tb + 1) * 512],
                                        start=(hk == 0), stop=(hk == NFH - 1))
                        return mm
                    blk_ev[(it, hb)] = p.step("tensor", blk, inc="sPE", waits=w)
                for sp in range(4):
                    ds = grp * 4 + sp
                    ob = ot[(it % 2) * 4 + sp]
                    w = [blk_ev[(it, 7)], yload[(it, sp)]]
                    if it >= 2:
                        w.append(oste[(it - 2, sp)])
                    res_ev[(it, sp)] = p.step("vector", (lambda v, sp=sp, ob=ob: v.tensor_tensor(
                        out=ob[:], in0=psD[sp][:], in1=yt[sp][:], op=ALU.add)),
                        inc="sDVE", waits=w)
                    oste[(it, sp)] = p.dma("gpsimd", out_fm[ds * 128:(ds + 1) * 128,
                                                           half * 1024:(half + 1) * 1024],
                                           ob[:], "qG", waits=[res_ev[(it, sp)]])
            p.step("gpsimd", lambda g: g.engine_nop(),
                   waits=p.queue_totals("qS") + p.queue_totals("qG"))

        block = es.enter_context(nc.Block())
        p.emit_all(block)
    build._last_prog = p
    return nc



def _hadamard(n):
    h = np.array([[1.0]], dtype=np.float32)
    while h.shape[0] < n:
        h = np.block([[h, h], [h, -h]])
    return h


_cc = {}


def _make_consts():
    if "c" not in _cc:
        import ml_dtypes
        c = np.zeros((128, 448), dtype=np.float32)
        k = np.arange(128)
        c[:, 0:128] = (k[:, None] <= k[None, :]).astype(np.float32)
        c[:, 128:256] = _hadamard(128)
        c[:, 256:384] = np.eye(128, dtype=np.float32)
        c[0:64, 384:448] = _hadamard(64)
        c[64:128, 384:448] = _hadamard(64)
        _cc["c"] = c.astype(ml_dtypes.bfloat16)
        kk = np.arange(128)
        _cc["mf"] = np.ascontiguousarray((kk[:, None] <= kk[None, :]).astype(np.float32))
    return _cc["c"], _cc["mf"]


def make_in_maps(L0_attn_x, L0_attn_Wq, L0_attn_bq, L0_attn_Wk, L0_attn_bk,
                 L0_attn_Wv, L0_attn_bv, L0_attn_Wo, L0_attn_bo,
                 L0_ffn_W1, L0_ffn_W2, L0_ffn_W3):
    x = np.asarray(L0_attn_x, dtype=np.float32)
    wq = np.ascontiguousarray(np.asarray(L0_attn_Wq, dtype=np.float32))
    wk = np.ascontiguousarray(np.asarray(L0_attn_Wk, dtype=np.float32))
    wv = np.ascontiguousarray(np.asarray(L0_attn_Wv, dtype=np.float32))
    wo = np.ascontiguousarray(np.asarray(L0_attn_Wo, dtype=np.float32))
    w1t = np.ascontiguousarray(np.asarray(L0_ffn_W1, dtype=np.float32).T)
    w3t = np.ascontiguousarray(np.asarray(L0_ffn_W3, dtype=np.float32).T)
    w2t = np.ascontiguousarray(np.asarray(L0_ffn_W2, dtype=np.float32).T)
    bq = np.asarray(L0_attn_bq, dtype=np.float32)
    bk = np.asarray(L0_attn_bk, dtype=np.float32)
    bv = np.asarray(L0_attn_bv, dtype=np.float32)
    bo = np.asarray(L0_attn_bo, dtype=np.float32)

    bcols = np.ascontiguousarray(np.concatenate(
        [bq.reshape(16, 128).T, bk.reshape(16, 128).T, bo.reshape(16, 128).T], axis=1))
    bvr = np.ascontiguousarray(bv[None, :])
    consts, maskf = _make_consts()
    epsc_v = np.full((1, 4), EPS_LN, dtype=np.float32)

    in_maps = []
    for c in range(8):
        b = c // 4
        s0 = (c % 4) * T
        xs = np.ascontiguousarray(x[b, s0:s0 + T, :].T)
        m8 = np.zeros((128, 8), dtype=np.float32)
        for r in range(8):
            if r // 4 == c // 4 and r < c:
                m8[:, r] = 1.0
        in_maps.append({
            "x_fm": xs, "wq": wq, "wk": wk, "wv": wv, "wo": wo,
            "w1t": w1t, "w3t": w3t, "w2t": w2t,
            "bcols": bcols, "bvrow": bvr, "consts": consts, "constf": maskf,
            "mask8": m8, "epsc": epsc_v,
        })
    return in_maps


def kernel(**inputs):
    in_maps = make_in_maps(**inputs)
    nc = build()
    res = run_bass_kernel_spmd(nc, in_maps, core_ids=list(range(8)))
    out = np.empty((2, 8192, D), dtype=np.float32)
    for c in range(8):
        b = c // 4
        s0 = (c % 4) * T
        out[b, s0:s0 + T, :] = res.results[c]["out_fm"].T
    return out


# version 09
# speedup vs baseline: 1.0133x; 1.0133x over previous
import numpy as np
from contextlib import ExitStack

import concourse.bass as bass
import concourse.mybir as mybir
from concourse.bass_utils import run_bass_kernel_spmd

F32 = mybir.dt.float32
BF16 = mybir.dt.bfloat16
AF = mybir.ActivationFunctionType
ALU = mybir.AluOpType

D = 2048
T = 2048
NDK = 16
NCH = 16
NH = 16
FH = 8192
NFH = 64
EPS_LN = 1e-5
EPS_AT = 1e-6
FWHT_SCALE = 1.0 / float(np.sqrt(FH))

ENGS = ("sync", "gpsimd", "tensor", "vector", "scalar")


class Prog:

    NQROT = 8

    def __init__(self, nc):
        self.nc = nc
        self.steps = {e: [] for e in ENGS}
        self.counts = {}
        self.sems = {}
        self.last_step = {e: None for e in ENGS}
        self.qrot = {}
        self.qnames = {}

    def sem(self, name):
        if name not in self.sems:
            self.sems[name] = self.nc.alloc_semaphore(name)
            self.counts[name] = 0
        return self.sems[name]

    def step(self, eng, emit, inc=None, inc_by=1, waits=(), is_dma=False):
        ev = None
        if inc is not None:
            self.sem(inc)
            self.counts[inc] += inc_by
            ev = (inc, self.counts[inc])
        s = {"emit": emit, "ev": ev, "inc_by": inc_by, "waits": [w for w in waits if w],
             "extra_inc": [], "is_dma": is_dma, "barred": False}
        self.steps[eng].append(s)
        self.last_step[eng] = s
        return ev

    def dma(self, eng, out, in_, q, waits=()):
        idx = self.qrot.get(q, 0)
        self.qrot[q] = idx + 1
        semname = f"{q}_{idx % self.NQROT}"
        if q not in self.qnames:
            self.qnames[q] = [f"{q}_{k}" for k in range(self.NQROT)]
            for sn in self.qnames[q]:
                self.sem(sn)

        def emit(h, out=out, in_=in_):
            return h.dma_start(out=out, in_=in_)
        return self.step(eng, emit, inc=semname, inc_by=16, waits=waits, is_dma=True)

    def queue_totals(self, q=None):
        qs = [q] if q else list(self.qnames)
        return [(sn, self.counts[sn]) for qq in qs for sn in self.qnames[qq]
                if self.counts[sn] > 0]

    def barrier(self):
        evs = []
        for e in ENGS:
            if self.last_step[e] is None:
                continue
            for st in reversed(self.steps[e]):
                if st["ev"] is not None:
                    evs.append(st["ev"])
                    break
            self.last_step[e] = None
        evs += self.queue_totals()
        evs = sorted(set(evs))
        for e in ENGS:
            self.step(e, None, waits=list(evs))

    def emit_all(self, block):
        for eng in ENGS:
            steps = self.steps[eng]
            if not steps:
                continue
            dec = getattr(block, eng)

            def section(h, steps=steps):
                for s in steps:
                    for (sn, v) in s["waits"]:
                        h.wait_ge(self.sems[sn], v)
                    ins = s["emit"](h) if s["emit"] is not None else None
                    if ins is not None:
                        if s["ev"] is not None:
                            ins.then_inc(self.sems[s["ev"][0]], s["inc_by"])
                        for (sn, by) in s["extra_inc"]:
                            ins.then_inc(self.sems[sn], by)
                    else:
                        assert s["ev"] is None, "eventful step with no instruction"
            dec(section)


def build(dump_scratch=False, upto="all", dump=()):
    nc = bass.Bass("TRN2", target_bir_lowering=False)
    def sk(name):
        if dump_scratch or name in dump:
            return "ExternalOutput"
        return "Internal"

    x_fm = nc.dram_tensor("x_fm", [D, T], F32, kind="ExternalInput").ap()
    wq = nc.dram_tensor("wq", [D, D], F32, kind="ExternalInput").ap()
    wk = nc.dram_tensor("wk", [D, D], F32, kind="ExternalInput").ap()
    wv = nc.dram_tensor("wv", [D, D], F32, kind="ExternalInput").ap()
    wo = nc.dram_tensor("wo", [D, D], F32, kind="ExternalInput").ap()
    w1t = nc.dram_tensor("w1t", [D, FH], F32, kind="ExternalInput").ap()
    w3t = nc.dram_tensor("w3t", [D, FH], F32, kind="ExternalInput").ap()
    w2t = nc.dram_tensor("w2t", [FH, D], F32, kind="ExternalInput").ap()
    bcols = nc.dram_tensor("bcols", [128, 48], F32, kind="ExternalInput").ap()
    bvrow = nc.dram_tensor("bvrow", [1, D], F32, kind="ExternalInput").ap()
    consts = nc.dram_tensor("consts", [128, 448], BF16, kind="ExternalInput").ap()
    constf = nc.dram_tensor("constf", [128, 128], F32, kind="ExternalInput").ap()
    mask8 = nc.dram_tensor("mask8", [128, 8], F32, kind="ExternalInput").ap()
    epsc = nc.dram_tensor("epsc", [1, 4], F32, kind="ExternalInput").ap()

    out_fm = nc.dram_tensor("out_fm", [D, T], F32, kind="ExternalOutput").ap()

    QS = nc.dram_tensor("QS", [NDK, 128, T], BF16, kind=sk("QS")).ap()
    KS = nc.dram_tensor("KS", [NDK, 128, T], BF16, kind=sk("KS")).ap()
    VS = nc.dram_tensor("VS", [NCH, 128, D], BF16, kind=sk("VS")).ap()
    KTS = nc.dram_tensor("KTS", [NCH, 128, D], BF16, kind=sk("KTS")).ap()
    YS = nc.dram_tensor("YS", [NDK, 128, T], F32, kind=sk("YS")).ap()
    HID = nc.dram_tensor("HID", [NFH, 128, T], BF16, kind=sk("HID")).ap()
    A2 = nc.dram_tensor("A2", [64, 128, D], BF16, kind=sk("A2")).ap()
    W2F = nc.dram_tensor("W2F", [NFH, 128, D], BF16, kind=sk("W2F")).ap()
    CCI = nc.dram_tensor("CCI", [128, 2064], F32, kind="Internal").ap()
    CCO = nc.dram_tensor("CCO", [8 * 128, 2064], F32, kind="Internal",
                         addr_space="Local").ap()
    XN_D = nc.dram_tensor("XN_D", [128, NDK * T], BF16, kind="ExternalOutput").ap() \
        if dump_scratch else None
    if dump_scratch:
        ATT4_D = nc.dram_tensor("ATT4_D", [128, NH * 512], BF16, kind="ExternalOutput").ap()
        NUM_D = nc.dram_tensor("NUM_D", [1, 128, D], F32, kind="ExternalOutput").ap()
        ZR_D = nc.dram_tensor("ZR_D", [1, 128, 16], F32, kind="ExternalOutput").ap()

    uid_c = [0]

    def uid():
        uid_c[0] += 1
        return uid_c[0]

    es = ExitStack()
    with es:
        sb = lambda name, shape, dt: es.enter_context(nc.sbuf_tensor(name, shape, dt))
        cst = sb("cst", [128, 448], BF16)
        maskf = sb("maskf", [128, 128], F32)
        ones_f32 = sb("ones_f32", [1, 128], F32)
        bcol_sb = sb("bcol_sb", [128, 48], F32)
        mask8_sb = sb("mask8_sb", [128, 8], F32)
        bv_bc = sb("bv_bc", [128, D], F32)
        eps_sb = sb("eps_sb", [1, 4], F32)
        state_kv = sb("state_kv", [128, NH * 128], F32)
        state_ks = sb("state_ks", [128, NH], F32)

        MASK = maskf[:]
        S128 = cst[:, 128:256]
        IDENT = cst[:, 256:384]
        S64 = cst[0:64, 384:448]
        S64H = cst[64:128, 384:448]
        ONES_COL = cst[:, 127:128]

        p = Prog(nc)
        for s_ in ("sPE", "sDVE", "sACT", "sPOOL", "sCC"):
            p.sem(s_)
        nc.const_aps.register(F32, EPS_LN, eps_sb[0:1, 0:1])

        with ExitStack() as pes:
            bvrow_sb = pes.enter_context(nc.sbuf_tensor("bvrow_sb", [1, D], F32))
            psBV = pes.enter_context(nc.psum_tensor("psBV", [128, D], F32))

            e1 = p.dma("sync", cst[:], consts, "qS")
            e2 = p.dma("sync", bcol_sb[:], bcols, "qS")
            e3 = p.dma("sync", mask8_sb[:], mask8, "qS")
            e4 = p.dma("sync", bvrow_sb[:], bvrow, "qS")
            e5 = p.dma("sync", maskf[:], constf, "qS")
            p.dma("sync", eps_sb[:], epsc, "qS")
            ev = p.step("vector", lambda v: v.memset(ones_f32[:], 1.0), inc="sDVE")

            def mk_bv(t):
                for tb in range(4):
                    mm = t.matmul(out=psBV[:, tb * 512:(tb + 1) * 512], lhsT=ones_f32[:],
                                  rhs=bvrow_sb[0:1, tb * 512:(tb + 1) * 512],
                                  start=True, stop=True)
                return mm
            epe = p.step("tensor", mk_bv, inc="sPE", waits=[e4, ev])
            p.step("scalar", lambda s: s.copy(out=bv_bc[:], in_=psBV[:]),
                   inc="sACT", waits=[epe])
            p.barrier()

        def ln_fm(src_getter, dst, chase=False):
            with ExitStack() as pes:
                xbf = pes.enter_context(nc.sbuf_tensor(f"ln_xbf{uid()}", [128, NDK * T], BF16))
                rows = pes.enter_context(nc.sbuf_tensor(f"ln_rows{uid()}", [1, 3 * T], F32))
                with ExitStack() as ses:
                    xt = [ses.enter_context(nc.sbuf_tensor(f"ln_xt{uid()}", [128, T], F32)) for i in range(2)]
                    sqbf = [ses.enter_context(nc.sbuf_tensor(f"ln_sq{uid()}", [128, T], BF16)) for i in range(2)]
                    psS = ses.enter_context(nc.psum_tensor(f"ln_psS{uid()}", [1, T], F32))
                    psQ = ses.enter_context(nc.psum_tensor(f"ln_psQ{uid()}", [1, T], F32))

                    ld, cast_e, sq_e, pe_e = [], [], [], []
                    for dk in range(NDK):
                        w = []
                        if dk >= 2:
                            w = [cast_e[dk - 2], sq_e[dk - 2]]
                        ld.append(p.dma("sync" if dk % 2 == 0 else "scalar",
                                        xt[dk % 2][:], src_getter(dk),
                                        "qS" if dk % 2 == 0 else "qA", waits=w))
                        cast_e.append(p.step(
                            "vector",
                            (lambda v, dk=dk: v.tensor_copy(out=xbf[:, dk * T:(dk + 1) * T],
                                                            in_=xt[dk % 2][:])),
                            inc="sDVE", waits=[ld[dk]]))
                        w = [ld[dk]] + ([pe_e[dk - 2]] if dk >= 2 else [])
                        sq_e.append(p.step(
                            "scalar",
                            (lambda s, dk=dk: s.activation(out=sqbf[dk % 2][:], in_=xt[dk % 2][:],
                                                           func=AF.Square)),
                            inc="sACT", waits=w))

                        def mm_stats(t, dk=dk):
                            for tb in range(4):
                                t.matmul(out=psS[0:1, tb * 512:(tb + 1) * 512], lhsT=ONES_COL,
                                         rhs=xbf[:, dk * T + tb * 512:dk * T + (tb + 1) * 512],
                                         start=(dk == 0), stop=(dk == NDK - 1))
                            for tb in range(4):
                                mm = t.matmul(out=psQ[0:1, tb * 512:(tb + 1) * 512], lhsT=ONES_COL,
                                              rhs=sqbf[dk % 2][:, tb * 512:(tb + 1) * 512],
                                              start=(dk == 0), stop=(dk == NDK - 1))
                            return mm
                        pe_e.append(p.step("tensor", mm_stats, inc="sPE",
                                           waits=[cast_e[dk], sq_e[dk]]))

                    def ev_rows(s):
                        s.copy(out=rows[0:1, 0:T], in_=psS[:])
                        return s.copy(out=rows[0:1, T:2 * T], in_=psQ[:])
                    er = p.step("scalar", ev_rows, inc="sACT", waits=[pe_e[-1]])

                    def rowmath_v(v):
                        v.tensor_scalar_mul(out=rows[0:1, 0:T],
                                            in0=rows[0:1, 0:T], scalar1=1.0 / T)
                        v.tensor_tensor(out=rows[0:1, 2 * T:3 * T], in0=rows[0:1, 0:T],
                                        in1=rows[0:1, 0:T], op=ALU.mult)
                        return v.scalar_tensor_tensor(
                            out=rows[0:1, T:2 * T], in0=rows[0:1, T:2 * T], scalar=1.0 / T,
                            in1=rows[0:1, 2 * T:3 * T], op0=ALU.mult, op1=ALU.subtract)
                    ev1 = p.step("vector", rowmath_v, inc="sDVE", waits=[er])

                    def rowmath_s(s):
                        s.activation(out=rows[0:1, T:2 * T], in_=rows[0:1, T:2 * T],
                                     func=AF.Ln, bias=EPS_LN)
                        return s.activation(out=rows[0:1, 2 * T:3 * T], in_=rows[0:1, T:2 * T],
                                            func=AF.Exp, scale=-0.5)
                    ev2 = p.step("scalar", rowmath_s, inc="sACT", waits=[ev1])

                    ev3 = p.step("vector", lambda v: v.scalar_tensor_tensor(
                        out=rows[0:1, T:2 * T], in0=rows[0:1, 0:T], scalar=-1.0,
                        in1=rows[0:1, 2 * T:3 * T], op0=ALU.mult, op1=ALU.mult),
                        inc="sDVE", waits=[ev2])
                    p.barrier()

                with ExitStack() as bes:
                    psBa = bes.enter_context(nc.psum_tensor(f"ln_psBa{uid()}", [128, T], F32))
                    psBb = bes.enter_context(nc.psum_tensor(f"ln_psBb{uid()}", [128, T], F32))
                    a_bc = bes.enter_context(nc.sbuf_tensor(f"ln_abc{uid()}", [128, T], F32))
                    b_bc = bes.enter_context(nc.sbuf_tensor(f"ln_bbc{uid()}", [128, T], F32))
                    tmpD = bes.enter_context(nc.sbuf_tensor(f"ln_tmpD{uid()}", [128, T], F32))
                    tmpP = bes.enter_context(nc.sbuf_tensor(f"ln_tmpP{uid()}", [128, T], F32))

                    def mk_bc(t):
                        for tb in range(4):
                            t.matmul(out=psBa[:, tb * 512:(tb + 1) * 512], lhsT=ones_f32[:],
                                     rhs=rows[0:1, 2 * T + tb * 512:2 * T + (tb + 1) * 512],
                                     start=True, stop=True)
                            mm = t.matmul(out=psBb[:, tb * 512:(tb + 1) * 512], lhsT=ones_f32[:],
                                          rhs=rows[0:1, T + tb * 512:T + (tb + 1) * 512],
                                          start=True, stop=True)
                        return mm
                    ebc = p.step("tensor", mk_bc, inc="sPE")

                    def ev_bc(s):
                        s.copy(out=a_bc[:], in_=psBa[:])
                        return s.copy(out=b_bc[:], in_=psBb[:])
                    eev = p.step("scalar", ev_bc, inc="sACT", waits=[ebc])

                    nrm = []
                    for dk in range(NDK):
                        dve = (dk % 8 < 5) and not chase or chase
                        eng = "vector" if dve else "gpsimd"
                        tb_ = tmpD if dve else tmpP

                        def nm(h, dk=dk, tb_=tb_):
                            h.tensor_tensor(out=tb_[:], in0=xbf[:, dk * T:(dk + 1) * T],
                                            in1=a_bc[:], op=ALU.mult)
                            return h.tensor_tensor(out=dst[:, dk * T:(dk + 1) * T],
                                                   in0=tb_[:], in1=b_bc[:], op=ALU.add)
                        nrm.append(p.step(eng, nm, inc=("sDVE" if dve else "sPOOL"),
                                          waits=[eev]))
                    p.barrier()
                    return nrm

        xn_es = ExitStack()
        xn = xn_es.enter_context(nc.sbuf_tensor("xn1", [128, NDK * T], BF16))
        nrm1 = ln_fm(lambda dk: x_fm[dk * 128:(dk + 1) * 128, :], xn, chase=False)
        if dump_scratch:
            p.dma("sync", XN_D, xn[:], "qS")
            p.barrier()

        def load_w_half(w_ap, half, dstbuf, waits=()):
            src = w_ap[:, half * 1024:(half + 1) * 1024].rearrange("(k p) o -> p k o", p=128)
            dst = dstbuf[:].rearrange("p (k o) -> p k o", o=1024)
            return p.dma("gpsimd", dst, src, "qG", waits=waits)

        def proj_fm(w_ap, bias_idx, dst_scr, xnl):
            with ExitStack() as pes:
                psP = [pes.enter_context(nc.psum_tensor(f"pj_ps{uid()}", [128, T], F32)) for i in range(2)]
                e0 = [pes.enter_context(nc.sbuf_tensor(f"pj_e0{uid()}", [128, T], F32)) for i in range(2)]
                rr = [pes.enter_context(nc.sbuf_tensor(f"pj_r{uid()}", [128, T], F32)) for i in range(2)]
                ot = [pes.enter_context(nc.sbuf_tensor(f"pj_o{uid()}", [128, T], BF16)) for i in range(2)]

                wl = [load_w_half(w_ap, 0, wbufA), load_w_half(w_ap, 1, wbufB)]
                pe_ev, act_ev, dve_ev, st_ev = [], [], [], []
                for ds in range(16):
                    half = ds // 8
                    wb = wbufA if half == 0 else wbufB
                    w = [wl[half]]
                    if ds >= 2:
                        w.append(act_ev[ds - 2])

                    def mms(t, ds=ds, wb=wb):
                        for dk in range(NDK):
                            for tb in range(4):
                                mm = t.matmul(
                                    out=psP[ds % 2][:, tb * 512:(tb + 1) * 512],
                                    lhsT=wb[:, dk * 1024 + (ds % 8) * 128: dk * 1024 + (ds % 8) * 128 + 128],
                                    rhs=xnl[:, dk * T + tb * 512: dk * T + (tb + 1) * 512],
                                    start=(dk == 0), stop=(dk == NDK - 1))
                        return mm
                    pe_ev.append(p.step("tensor", mms, inc="sPE", waits=w))

                    w = [pe_ev[ds]]
                    if ds >= 2:
                        w.append(dve_ev[ds - 2])

                    def act(s, ds=ds):
                        bias = bcol_sb[:, bias_idx * 16 + ds: bias_idx * 16 + ds + 1]
                        s.activation(out=e0[ds % 2][:], in_=psP[ds % 2][:], func=AF.Exp, bias=bias)
                        return s.activation(out=rr[ds % 2][:], in_=psP[ds % 2][:], func=AF.Relu, bias=bias)
                    act_ev.append(p.step("scalar", act, inc="sACT", waits=w))

                    w = [act_ev[ds]]
                    if ds >= 2:
                        w.append(st_ev[ds - 2])
                    dve_ev.append(p.step("vector", (lambda v, ds=ds: v.scalar_tensor_tensor(
                        out=ot[ds % 2][:], in0=e0[ds % 2][:], scalar=1.0, in1=rr[ds % 2][:],
                        op0=ALU.min, op1=ALU.add)), inc="sDVE", waits=w))
                    st_ev.append(p.dma("sync", dst_scr[ds], ot[ds % 2][:], "qS",
                                       waits=[dve_ev[ds]]))
                p.barrier()

        def proj_tm_v(xnl, nrm):
            with ExitStack() as pes:
                psP = [pes.enter_context(nc.psum_tensor(f"pv_ps{i}", [128, D], F32)) for i in range(2)]
                ot = [pes.enter_context(nc.sbuf_tensor(f"pv_o{i}", [128, D], BF16)) for i in range(2)]
                wl = [load_w_half(wv, 0, wbufA), load_w_half(wv, 1, wbufB)]
                pe_ev, dve_ev, st_ev = [], [], []
                for ts_ in range(16):
                    w = [wl[0], wl[1]]
                    if ts_ >= 2:
                        w.append(dve_ev[ts_ - 2])

                    if ts_ == 0:
                        for dk in range(NDK):
                            def mmc(t, dk=dk):
                                for dq in range(4):
                                    wb = wbufA if dq < 2 else wbufB
                                    off = (dq % 2) * 512
                                    mm = t.matmul(
                                        out=psP[0][:, dq * 512:(dq + 1) * 512],
                                        lhsT=xnl[:, dk * T: dk * T + 128],
                                        rhs=wb[:, dk * 1024 + off: dk * 1024 + off + 512],
                                        start=(dk == 0), stop=(dk == NDK - 1))
                                return mm
                            ev = p.step("tensor", mmc, inc="sPE", waits=w + [nrm[dk]])
                            w = []
                        pe_ev.append(ev)
                    else:
                        def mms(t, ts_=ts_):
                            for dk in range(NDK):
                                for dq in range(4):
                                    wb = wbufA if dq < 2 else wbufB
                                    off = (dq % 2) * 512
                                    mm = t.matmul(
                                        out=psP[ts_ % 2][:, dq * 512:(dq + 1) * 512],
                                        lhsT=xnl[:, dk * T + ts_ * 128: dk * T + ts_ * 128 + 128],
                                        rhs=wb[:, dk * 1024 + off: dk * 1024 + off + 512],
                                        start=(dk == 0), stop=(dk == NDK - 1))
                            return mm
                        pe_ev.append(p.step("tensor", mms, inc="sPE", waits=w))
                    w = [pe_ev[ts_]]
                    if ts_ >= 2:
                        w.append(st_ev[ts_ - 2])
                    dve_ev.append(p.step("vector", (lambda v, ts_=ts_: v.tensor_tensor(
                        out=ot[ts_ % 2][:], in0=psP[ts_ % 2][:], in1=bv_bc[:], op=ALU.add)),
                        inc="sDVE", waits=w))
                    st_ev.append(p.dma("sync", VS[ts_], ot[ts_ % 2][:], "qS", waits=[dve_ev[ts_]]))
                p.barrier()

        wbuf_es = ExitStack()
        wbufA = wbuf_es.enter_context(nc.sbuf_tensor("wbufA", [128, NDK * 1024], BF16, side="right"))
        wbufB = wbuf_es.enter_context(nc.sbuf_tensor("wbufB", [128, NDK * 1024], BF16, side="right"))
        proj_tm_v(xn, nrm1)
        proj_fm(wk, 1, KS, xn)
        proj_fm(wq, 0, QS, xn)
        xn_es.close()
        if upto == "p2":
            wbuf_es.close()
            p.emit_all(es.enter_context(nc.Block()))
            build._last_prog = p
            return nc

        with ExitStack() as pes:
            kfn3 = [pes.enter_context(nc.sbuf_tensor(f"p3kf{i}", [128, D], BF16)) for i in range(2)]
            vcn3 = [pes.enter_context(nc.sbuf_tensor(f"p3vc{i}", [128, D], BF16)) for i in range(2)]
            ktm = [pes.enter_context(nc.sbuf_tensor(f"p3_kt{i}", [128, D], BF16)) for i in range(2)]
            t_sb = pes.enter_context(nc.sbuf_tensor("p3_T", [128, 2064], F32))
            psT = pes.enter_context(nc.psum_tensor("p3_psT", [128, 1024], BF16))
            psSU = [pes.enter_context(nc.psum_tensor(f"p3_su{i}", [128, 512], F32)) for i in range(2)]
            psKS = pes.enter_context(nc.psum_tensor("p3_ks", [128, 16], F32))

            zero_ev = p.step("vector", lambda v: (v.memset(state_kv[:], 0.0),
                                                  v.memset(state_ks[:], 0.0))[-1], inc="sDVE")
            ld_k, ld_v, tr_ev, ev_ev, su_ev, add_ev, ks_ev = [], [], [], [], [], [], []
            chunk_pe_last = []
            for n in range(NCH):
                w = [chunk_pe_last[n - 2]] if n >= 2 else []
                src = KS.rearrange("s p t -> p s t")[:, :, n * 128:(n + 1) * 128]
                ld_k.append(p.dma("sync", kfn3[n % 2][:].rearrange("p (s t) -> p s t", t=128),
                                  src, "qS", waits=w))
                ld_v.append(p.dma("scalar", vcn3[n % 2][:], VS[n], "qA", waits=w))
                for hg in range(4):
                    gidx = n * 4 + hg
                    w = [ld_k[n]]
                    if gidx >= 1:
                        w.append(ev_ev[gidx - 1])

                    def trs(t, n=n, hg=hg):
                        for hh in range(4):
                            h = hg * 4 + hh
                            mm = t.transpose(
                                out=psT[:, (hg % 2) * 512 + hh * 128:(hg % 2) * 512 + hh * 128 + 128],
                                in_=kfn3[n % 2][:, h * 128:(h + 1) * 128], identity=IDENT)
                        return mm
                    tr_ev.append(p.step("tensor", trs, inc="sPE", waits=w))
                    ev_ev.append(p.step("scalar", (lambda s, n=n, hg=hg: s.activation(
                        out=ktm[n % 2][:, hg * 512:(hg + 1) * 512],
                        in_=psT[:, (hg % 2) * 512:(hg % 2) * 512 + 512], func=AF.Copy)),
                        inc="sACT", waits=[tr_ev[gidx]]))
                    w = [ev_ev[gidx], ld_v[n]]
                    if gidx >= 2:
                        w.append(add_ev[gidx - 2])
                    if hg == 0 and n >= 1:
                        w.append(ks_ev[n - 1])

                    def sus(t, n=n, hg=hg):
                        for hh in range(4):
                            h = hg * 4 + hh
                            t.matmul(out=psSU[hg % 2][:, hh * 128:(hh + 1) * 128],
                                     lhsT=ktm[n % 2][:, h * 128:(h + 1) * 128],
                                     rhs=vcn3[n % 2][:, h * 128:(h + 1) * 128],
                                     start=True, stop=True)
                        for hh in range(4):
                            h = hg * 4 + hh
                            mm = t.matmul(out=psKS[:, h:h + 1],
                                          lhsT=ktm[n % 2][:, h * 128:(h + 1) * 128],
                                          rhs=ONES_COL, start=True, stop=True)
                        return mm
                    su_ev.append(p.step("tensor", sus, inc="sPE", waits=w))
                    add_ev.append(p.step("vector", (lambda v, hg=hg: v.tensor_tensor(
                        out=state_kv[:, hg * 512:(hg + 1) * 512],
                        in0=state_kv[:, hg * 512:(hg + 1) * 512],
                        in1=psSU[hg % 2][:], op=ALU.add)),
                        inc="sDVE", waits=[su_ev[gidx]] + ([zero_ev] if gidx == 0 else [])))
                ks_ev.append(p.step("vector", lambda v: v.tensor_tensor(
                    out=state_ks[:], in0=state_ks[:], in1=psKS[:], op=ALU.add),
                    inc="sDVE", waits=[su_ev[n * 4 + 3]]))
                chunk_pe_last.append(su_ev[n * 4 + 3])
                p.dma("sync", KTS[n], ktm[n % 2][:], "qS", waits=[ev_ev[n * 4 + 3]])

            cpy = p.step("vector", lambda v: (v.tensor_copy(out=t_sb[:, 0:2048], in_=state_kv[:]),
                                              v.tensor_copy(out=t_sb[:, 2048:2064], in_=state_ks[:]))[-1],
                         inc="sDVE", waits=[add_ev[-1], ks_ev[-1]])
            wo_l = [load_w_half(wo, 0, wbufA), load_w_half(wo, 1, wbufB)]

            w2a = pes.enter_context(nc.sbuf_tensor("w2a", [64, D], BF16))
            w2a2 = pes.enter_context(nc.sbuf_tensor("w2a2", [64, D], BF16))
            w2o = pes.enter_context(nc.sbuf_tensor("w2o", [64, D], BF16))
            w2o2 = pes.enter_context(nc.sbuf_tensor("w2o2", [64, D], BF16))
            psA_ = pes.enter_context(nc.psum_tensor("w2psA", [64, D], F32))
            wa_ld, wa_pe, wa_ev, wa_st = [], [], [], []
            w2t_j = w2t.rearrange("(i j) d -> j i d", j=128)
            for j2 in range(128):
                w = [wa_pe[j2 - 2]] if j2 >= 2 else []
                wa_ld.append(p.dma("gpsimd", (w2a if j2 % 2 == 0 else w2a2)[:],
                                   w2t_j[j2], "qG", waits=w))
                w = [wa_ld[j2]] + ([wa_ev[j2 - 1]] if j2 >= 1 else [])

                def mm_a(t, j2=j2):
                    for tb in range(4):
                        mm = t.matmul(out=psA_[:, tb * 512:(tb + 1) * 512], lhsT=S64,
                                      rhs=(w2a if j2 % 2 == 0 else w2a2)[:, tb * 512:(tb + 1) * 512],
                                      start=True, stop=True)
                    return mm
                wa_pe.append(p.step("tensor", mm_a, inc="sPE", waits=w))
                w = [wa_pe[j2]] + ([wa_st[j2 - 2]] if j2 >= 2 else [])
                wa_ev.append(p.step("scalar", (lambda s, j2=j2: s.activation(
                    out=(w2o if j2 % 2 == 0 else w2o2)[:], in_=psA_[:], func=AF.Copy)),
                    inc="sACT", waits=w))
                wa_st.append(p.dma("sync", A2.rearrange("i j d -> j i d")[j2],
                                   (w2o if j2 % 2 == 0 else w2o2)[:], "qS",
                                   waits=[wa_ev[j2]]))
            eci = p.dma("gpsimd", CCI, t_sb[:], "qG", waits=[cpy])
            p.step("gpsimd", lambda g: g.collective_compute(
                "AllGather", ALU.bypass, ins=[CCI.opt()], outs=[CCO.opt()],
                replica_groups=[list(range(8))]), inc="sCC", waits=[eci])
            p.last_step["gpsimd"] = None
            p.barrier()

        if upto == "p3":
            wbuf_es.close()
            p.emit_all(es.enter_context(nc.Block()))
            build._last_prog = p
            return nc

        with ExitStack() as pes:
            a2i = [pes.enter_context(nc.sbuf_tensor(f"a2i{i}", [128, 2 * D], BF16)) for i in range(2)]
            w2ob = [pes.enter_context(nc.sbuf_tensor(f"w2fo{i}", [128, D], BF16)) for i in range(2)]
            psB_ = [pes.enter_context(nc.psum_tensor(f"w2psB{i}", [128, D], F32)) for i in range(2)]
            ld, pe_ev, act_ev, st_ev = [], [], [], []
            for i in range(64):
                ip, ii = i // 2, i % 2
                if ii == 0:
                    w = [pe_ev[i - 3]] if i >= 3 else []
                    srcap = A2.rearrange("i p d -> p i d")[:, 2 * ip:2 * ip + 2, :]
                    ld.append(p.dma("sync", a2i[ip % 2][:].rearrange("p (i d) -> p i d", d=D),
                                    srcap, "qS", waits=w))
                w = [ld[ip]] + ([act_ev[i - 2]] if i >= 2 else [])

                def mm_b(t, ip=ip, ii=ii, i=i):
                    for tb in range(4):
                        mm = t.matmul(out=psB_[i % 2][:, tb * 512:(tb + 1) * 512],
                                      lhsT=S128,
                                      rhs=a2i[ip % 2][:, ii * D + tb * 512: ii * D + (tb + 1) * 512],
                                      start=True, stop=True)
                    return mm
                pe_ev.append(p.step("tensor", mm_b, inc="sPE", waits=w))
                w = [pe_ev[i]] + ([st_ev[i - 2]] if i >= 2 else [])
                act_ev.append(p.step("scalar", (lambda s, i=i: s.activation(
                    out=w2ob[i % 2][:], in_=psB_[i % 2][:], func=AF.Copy,
                    scale=FWHT_SCALE)), inc="sACT", waits=w))
                st_ev.append(p.dma("scalar", W2F[i], w2ob[i % 2][:], "qA",
                                   waits=[act_ev[i]]))
            p.barrier()

        with ExitStack() as pes:
            pf = [pes.enter_context(nc.sbuf_tensor(f"pf{i}", [128, 2064], F32)) for i in range(2)]
            z_ev = p.step("vector", lambda v: (v.memset(state_kv[:], 0.0),
                                               v.memset(state_ks[:], 0.0))[-1], inc="sDVE")
            ld, ad = [], []
            for r in range(8):
                w = [("sCC", p.counts["sCC"])] if r == 0 else []
                if r >= 2:
                    w.append(ad[r - 2])
                ld.append(p.dma("sync", pf[r % 2][:], CCO[r * 128:(r + 1) * 128, :], "qS",
                                waits=w))
                def addpf(v, r=r):
                    v.scalar_tensor_tensor(out=state_kv[:], in0=pf[r % 2][:, 0:2048],
                                           scalar=mask8_sb[:, r:r + 1], in1=state_kv[:],
                                           op0=ALU.mult, op1=ALU.add)
                    return v.scalar_tensor_tensor(out=state_ks[:], in0=pf[r % 2][:, 2048:2064],
                                                  scalar=mask8_sb[:, r:r + 1], in1=state_ks[:],
                                                  op0=ALU.mult, op1=ALU.add)
                ad.append(p.step("vector", addpf, inc="sDVE",
                                 waits=[ld[r]] + ([z_ev] if r == 0 else [])))
            p.barrier()

        if upto == "pw2":
            wbuf_es.close()
            p.emit_all(es.enter_context(nc.Block()))
            build._last_prog = p
            return nc

        with ExitStack() as pes:
            qfn = [pes.enter_context(nc.sbuf_tensor(f"p4_qf{i}", [128, D], BF16)) for i in range(2)]
            kfn = [pes.enter_context(nc.sbuf_tensor(f"p4_kf{i}", [128, D], BF16)) for i in range(2)]
            ktn = [pes.enter_context(nc.sbuf_tensor(f"p4_kt{i}", [128, D], BF16)) for i in range(2)]
            vcn = [pes.enter_context(nc.sbuf_tensor(f"p4_vc{i}", [128, D], BF16)) for i in range(2)]
            sbf_kv = pes.enter_context(nc.sbuf_tensor("p4_skv", [128, D], BF16))
            sbf_ks = pes.enter_context(nc.sbuf_tensor("p4_sks", [128, 16], BF16))
            atm = pes.enter_context(nc.sbuf_tensor("p4_atm", [128, D], BF16))
            num_sb = pes.enter_context(nc.sbuf_tensor("p4_num", [128, D], F32))
            ztmp = pes.enter_context(nc.sbuf_tensor("p4_z", [128, 16], F32))
            zr = pes.enter_context(nc.sbuf_tensor("p4_zr", [128, 16], F32))
            att_tm = pes.enter_context(nc.sbuf_tensor("p4_att", [128, D], BF16))
            att_fm16 = pes.enter_context(nc.sbuf_tensor("p4_af16", [128, NH * T], BF16))
            xt4 = [pes.enter_context(nc.sbuf_tensor(f"p4_xt{i}", [128, 512], F32)) for i in range(2)]
            y_sb = [pes.enter_context(nc.sbuf_tensor(f"p4_y{i}", [128, 512], F32)) for i in range(2)]

            B = [pes.enter_context(nc.psum_tensor(f"p4_B{i}", [128, 512], F32)) for i in range(6)]
            psZK = pes.enter_context(nc.psum_tensor("p4_zk", [128, 48], F32))
            psTT = pes.enter_context(nc.psum_tensor("p4_tt", [128, 1024], BF16))
            maskbc = bass.AP(maskf[:].tensor, 0, [[128, 128], [0, 4], [1, 128]])
            zrbc = bass.AP(zr[:].tensor, 0, [[16, 128], [1, 16], [0, 128]])

            chunk_done = []
            at_ev, mask_ev, su_ev, num_ev, z_ev = {}, {}, {}, {}, {}
            stadd_ev, nev_ev, trg_ev, tev_ev, oproj_ev, stt_ev, yst_ev = {}, {}, {}, {}, {}, {}, {}
            cast_ev, zeps_ev, zr_ev, ksadd_ev, div_ev = [], [], [], [], []

            for n in range(NCH):
                g4 = n // 4
                w_buf = [chunk_done[n - 2]] if n >= 2 else []
                lq = p.dma("sync", qfn[n % 2][:].rearrange("p (s t) -> p s t", t=128),
                           QS.rearrange("s p t -> p s t")[:, :, n * 128:(n + 1) * 128],
                           "qS", waits=w_buf)
                lk = p.dma("sync", kfn[n % 2][:].rearrange("p (s t) -> p s t", t=128),
                           KS.rearrange("s p t -> p s t")[:, :, n * 128:(n + 1) * 128],
                           "qS", waits=w_buf)
                lt = p.dma("scalar", ktn[n % 2][:], KTS[n], "qA", waits=w_buf)
                lv = p.dma("scalar", vcn[n % 2][:], VS[n], "qA", waits=w_buf)

                w = ([stadd_ev[(n - 1, g_)] for g_ in range(4)] + [ksadd_ev[n - 1]]) if n >= 1 else []
                cast_ev.append(p.step("scalar", lambda s: (
                    s.activation(out=sbf_kv[:], in_=state_kv[:], func=AF.Copy),
                    s.activation(out=sbf_ks[:], in_=state_ks[:], func=AF.Copy))[-1],
                    inc="sACT", waits=w))

                for g in range(4):
                    w = [lq, lk]
                    if n >= 1:
                        w.append(mask_ev[(n - 1, g)])

                    def ats(t, n=n, g=g):
                        for j in range(4):
                            h = g * 4 + j
                            mm = t.matmul(out=B[g][:, j * 128:(j + 1) * 128],
                                          lhsT=kfn[n % 2][:, h * 128:(h + 1) * 128],
                                          rhs=qfn[n % 2][:, h * 128:(h + 1) * 128],
                                          start=True, stop=True)
                        return mm
                    at_ev[(n, g)] = p.step("tensor", ats, inc="sPE", waits=w)

                for g in range(4):
                    w = [at_ev[(n, g)]]
                    if n >= 1:
                        w.append(z_ev[(n - 1, g)])

                    def msk(v, g=g):
                        return v.tensor_tensor(
                            out=atm[:, g * 512:(g + 1) * 512].rearrange("p (j e) -> p j e", e=128),
                            in0=B[g][:].rearrange("p (j e) -> p j e", e=128),
                            in1=maskbc, op=ALU.mult)
                    mask_ev[(n, g)] = p.step("vector", msk, inc="sDVE", waits=w)

                for g in range(4):
                    w = [lt, lv]
                    w.append(stadd_ev[(n, g - 1)] if g >= 1 else
                             (stadd_ev[(n - 1, 3)] if n >= 1 else None))

                    def sus(t, n=n, g=g):
                        for j in range(4):
                            h = g * 4 + j
                            mm = t.matmul(out=B[4][:, j * 128:(j + 1) * 128],
                                          lhsT=ktn[n % 2][:, h * 128:(h + 1) * 128],
                                          rhs=vcn[n % 2][:, h * 128:(h + 1) * 128],
                                          start=True, stop=True)
                        return mm
                    su_ev[(n, g)] = p.step("tensor", sus, inc="sPE",
                                           waits=[x for x in w if x])
                    stadd_ev[(n, g)] = p.step("vector", (lambda v, g=g: v.tensor_tensor(
                        out=state_kv[:, g * 512:(g + 1) * 512],
                        in0=state_kv[:, g * 512:(g + 1) * 512],
                        in1=B[4][:], op=ALU.add)),
                        inc="sDVE", waits=[su_ev[(n, g)], cast_ev[n]])

                    w = [cast_ev[n], lv, mask_ev[(n, g)]]
                    w.append(nev_ev[(n, g - 1)] if g >= 1 else
                             (nev_ev[(n - 1, 3)] if n >= 1 else None))

                    def nums(t, n=n, g=g):
                        for j in range(4):
                            h = g * 4 + j
                            t.matmul(out=B[5][:, j * 128:(j + 1) * 128],
                                     lhsT=qfn[n % 2][:, h * 128:(h + 1) * 128],
                                     rhs=sbf_kv[:, h * 128:(h + 1) * 128],
                                     start=True, stop=False)
                            mm = t.matmul(out=B[5][:, j * 128:(j + 1) * 128],
                                          lhsT=atm[:, h * 128:(h + 1) * 128],
                                          rhs=vcn[n % 2][:, h * 128:(h + 1) * 128],
                                          start=False, stop=True)
                        return mm
                    num_ev[(n, g)] = p.step("tensor", nums, inc="sPE",
                                            waits=[x for x in w if x])

                    w = [cast_ev[n], lt, mask_ev[(n, g)]]
                    if g == 0 and n >= 1:
                        w += [zeps_ev[n - 1], ksadd_ev[n - 1]]

                    def zs(t, n=n, g=g):
                        for j in range(4):
                            h = g * 4 + j
                            t.matmul(out=psZK[:, h:h + 1],
                                     lhsT=qfn[n % 2][:, h * 128:(h + 1) * 128],
                                     rhs=sbf_ks[:, h:h + 1], start=True, stop=False)
                            t.matmul(out=psZK[:, h:h + 1],
                                     lhsT=atm[:, h * 128:(h + 1) * 128],
                                     rhs=ONES_COL, start=False, stop=True)
                            mm = t.matmul(out=psZK[:, 16 + h:16 + h + 1],
                                          lhsT=ktn[n % 2][:, h * 128:(h + 1) * 128],
                                          rhs=ONES_COL, start=True, stop=True)
                        return mm
                    z_ev[(n, g)] = p.step("tensor", zs, inc="sPE",
                                          waits=[x for x in w if x])

                    w = [num_ev[(n, g)]]
                    if n >= 1:
                        w.append(div_ev[n - 1])
                    nev_ev[(n, g)] = p.step("scalar", (lambda s, g=g: s.activation(
                        out=num_sb[:, g * 512:(g + 1) * 512], in_=B[5][:], func=AF.Copy)),
                        inc="sACT", waits=w)

                zeps_ev.append(p.step("vector", lambda v: v.tensor_scalar_add(
                    out=ztmp[:], in0=psZK[:, 0:16], scalar1=EPS_AT),
                    inc="sDVE", waits=[z_ev[(n, 3)]]))
                zr_ev.append(p.step("vector", lambda v: v.reciprocal(out=zr[:], in_=ztmp[:]),
                                    inc="sDVE", waits=[zeps_ev[n]]))
                ksadd_ev.append(p.step("vector", lambda v: v.tensor_tensor(
                    out=state_ks[:], in0=state_ks[:], in1=psZK[:, 16:32], op=ALU.add),
                    inc="sDVE", waits=[z_ev[(n, 3)], cast_ev[n]]))
                w = [zr_ev[n]] + [nev_ev[(n, g_)] for g_ in range(4)]
                if n >= 1:
                    w.append(trg_ev[(n - 1, 3)])

                def divs(v):
                    return v.tensor_tensor(
                        out=att_tm[:].rearrange("p (h e) -> p h e", e=128),
                        in0=num_sb[:].rearrange("p (h e) -> p h e", e=128),
                        in1=zrbc, op=ALU.mult)
                div_ev.append(p.step("vector", divs, inc="sDVE", waits=w))

                for g in range(4):
                    w = [div_ev[n]]
                    gidx = n * 4 + g
                    if gidx >= 1:
                        w.append(tev_ev[(n, g - 1)] if g >= 1 else tev_ev[(n - 1, 3)])

                    def trp(t, n=n, g=g):
                        for j in range(4):
                            h = g * 4 + j
                            mm = t.transpose(
                                out=psTT[:, (g % 2) * 512 + j * 128:(g % 2) * 512 + j * 128 + 128],
                                in_=att_tm[:, h * 128:(h + 1) * 128], identity=IDENT)
                        return mm
                    trg_ev[(n, g)] = p.step("tensor", trp, inc="sPE", waits=w)

                    def tev(s, n=n, g=g):
                        for j in range(4):
                            h = g * 4 + j
                            ins = s.activation(
                                out=att_fm16[:, h * T + n * 128: h * T + n * 128 + 128],
                                in_=psTT[:, (g % 2) * 512 + j * 128:(g % 2) * 512 + j * 128 + 128],
                                func=AF.Copy)
                        return ins
                    tev_ev[(n, g)] = p.step("scalar", tev, inc="sACT", waits=[trg_ev[(n, g)]])
                chunk_done.append(trg_ev[(n, 3)])

            sweep_w0 = [tev_ev[(NCH - 1, g_)] for g_ in range(4)]
            stt_fl, yst_fl = [], []
            for m in range(64):
                g4, ds = m // 16, m % 16
                w = [stt_fl[m - 2]] if m >= 2 else []
                lx = p.dma("sync", xt4[m % 2][:],
                           x_fm[ds * 128:(ds + 1) * 128, g4 * 512:(g4 + 1) * 512],
                           "qS", waits=w)
                half = ds // 8
                wb = wbufA if half == 0 else wbufB
                w = list(sweep_w0) if m < 4 else [stt_fl[m - 4]]
                w.append(wo_l[half])

                def opj(t, g4=g4, ds=ds, m=m, wb=wb):
                    for ek in range(NH):
                        mm = t.matmul(
                            out=B[m % 4][:],
                            lhsT=wb[:, ek * 1024 + (ds % 8) * 128: ek * 1024 + (ds % 8) * 128 + 128],
                            rhs=att_fm16[:, ek * T + g4 * 512:(ek * T) + (g4 + 1) * 512],
                            start=(ek == 0), stop=(ek == NH - 1))
                    return mm
                oev = p.step("tensor", opj, inc="sPE", waits=w)
                w = [oev, lx]
                if m >= 2:
                    w.append(yst_fl[m - 2])
                stt_fl.append(p.step("vector", (lambda v, ds=ds, m=m: v.scalar_tensor_tensor(
                    out=y_sb[m % 2][:], in0=B[m % 4][:],
                    scalar=bcol_sb[:, 32 + ds:32 + ds + 1], in1=xt4[m % 2][:],
                    op0=ALU.add, op1=ALU.add)),
                    inc="sDVE", waits=w))
                yst_fl.append(p.dma("sync", YS[ds][:, g4 * 512:(g4 + 1) * 512],
                                    y_sb[m % 2][:], "qS", waits=[stt_fl[m]]))
            p.barrier()

        if upto == "p4":
            wbuf_es.close()
            p.emit_all(es.enter_context(nc.Block()))
            build._last_prog = p
            return nc

        wbuf_es.close()
        xn = es.enter_context(nc.sbuf_tensor("xn2", [128, NDK * T], BF16))
        nrm2 = ln_fm(lambda dk: YS[dk], xn, chase=False)

        with ExitStack() as pes:
            w1b = [pes.enter_context(nc.sbuf_tensor(f"p6w1{i}", [128, NDK * 128], BF16)) for i in range(2)]
            w3b = [pes.enter_context(nc.sbuf_tensor(f"p6w3{i}", [128, NDK * 128], BF16)) for i in range(2)]
            sg = pes.enter_context(nc.sbuf_tensor("p6_sg", [128, T], F32))
            hid = [pes.enter_context(nc.sbuf_tensor(f"p6_h{i}", [128, T], BF16)) for i in range(2)]
            psUg = pes.enter_context(nc.psum_tensor("p6_psA", [128, T], F32))
            psUl = pes.enter_context(nc.psum_tensor("p6_psB", [128, T], F32))

            l1, l3, g_ev, l_ev, si_ev, tt_ev, st_ev = [], [], [], [], [], [], []
            for hs in range(NFH):
                w = [g_ev[hs - 2], l_ev[hs - 2]] if hs >= 2 else []
                srcap = w1t[:, hs * 128:(hs + 1) * 128].rearrange("(k p) h -> p k h", p=128)
                l1.append(p.dma("gpsimd", w1b[hs % 2][:].rearrange("p (k h) -> p k h", h=128),
                                srcap, "qG", waits=w))
                srcap = w3t[:, hs * 128:(hs + 1) * 128].rearrange("(k p) h -> p k h", p=128)
                l3.append(p.dma("gpsimd", w3b[hs % 2][:].rearrange("p (k h) -> p k h", h=128),
                                srcap, "qG", waits=w))
                w = [l1[hs]] + ([si_ev[hs - 1]] if hs >= 1 else [])

                def mm_g(t, hs=hs, xnl=xn):
                    for dk in range(NDK):
                        for tb in range(4):
                            mm = t.matmul(out=psUg[:, tb * 512:(tb + 1) * 512],
                                          lhsT=w1b[hs % 2][:, dk * 128:(dk + 1) * 128],
                                          rhs=xnl[:, dk * T + tb * 512: dk * T + (tb + 1) * 512],
                                          start=(dk == 0), stop=(dk == NDK - 1))
                    return mm
                g_ev.append(p.step("tensor", mm_g, inc="sPE", waits=w))
                w = [l3[hs]] + ([tt_ev[hs - 1]] if hs >= 1 else [])

                def mm_l(t, hs=hs, xnl=xn):
                    for dk in range(NDK):
                        for tb in range(4):
                            mm = t.matmul(out=psUl[:, tb * 512:(tb + 1) * 512],
                                          lhsT=w3b[hs % 2][:, dk * 128:(dk + 1) * 128],
                                          rhs=xnl[:, dk * T + tb * 512: dk * T + (tb + 1) * 512],
                                          start=(dk == 0), stop=(dk == NDK - 1))
                    return mm
                l_ev.append(p.step("tensor", mm_l, inc="sPE", waits=w))
                si_ev.append(p.step("scalar", (lambda s, hs=hs: s.activation(
                    out=sg[:], in_=psUg[:], func=AF.Silu)), inc="sACT",
                    waits=[g_ev[hs]] + ([tt_ev[hs - 1]] if hs >= 1 else [])))
                w = [l_ev[hs], si_ev[hs]] + ([st_ev[hs - 2]] if hs >= 2 else [])
                tt_ev.append(p.step("vector", (lambda v, hs=hs: v.tensor_tensor(
                    out=hid[hs % 2][:], in0=sg[:], in1=psUl[:], op=ALU.mult)),
                    inc="sDVE", waits=w))
                st_ev.append(p.dma("sync", HID[hs], hid[hs % 2][:], "qS", waits=[tt_ev[hs]]))
            p.barrier()

        if upto == "p6a":
            p.emit_all(es.enter_context(nc.Block()))
            build._last_prog = p
            return nc

        with ExitStack() as pes:
            w2fb = [pes.enter_context(nc.sbuf_tensor(f"p6d_w{i}", [128, 8 * 512], BF16)) for i in range(3)]
            hid8 = [pes.enter_context(nc.sbuf_tensor(f"p6d_h{i}", [128, 8 * 1024], BF16)) for i in range(3)]
            yt = [pes.enter_context(nc.sbuf_tensor(f"p6d_y{i}", [128, 1024], F32)) for i in range(4)]
            ot = [pes.enter_context(nc.sbuf_tensor(f"p6d_o{i}", [128, 1024], F32)) for i in range(8)]
            psD = [pes.enter_context(nc.psum_tensor(f"p6d_ps{i}", [128, 1024], F32)) for i in range(4)]

            hload, wload, yload, blk_ev, res_ev, oste = {}, {}, {}, {}, {}, {}
            for it in range(8):
                half, grp = it // 4, it % 4
                for hb in range(8):
                    gi = it * 8 + hb
                    w = []
                    if gi >= 3:
                        prev = (it, hb - 3) if hb >= 3 else (it - 1, hb + 5)
                        w.append(blk_ev[prev])
                    eng = "sync" if hb % 2 == 0 else "scalar"
                    q = "qS" if hb % 2 == 0 else "qA"
                    hload[(it, hb)] = p.dma(
                        eng, hid8[gi % 3][:].rearrange("p (c t) -> p c t", t=1024),
                        HID.rearrange("c p t -> p c t")[:, 8 * hb:8 * hb + 8,
                                                        half * 1024:(half + 1) * 1024],
                        q, waits=w)
                    wload[(it, hb)] = p.dma(
                        "gpsimd", w2fb[gi % 3][:].rearrange("p (c e) -> p c e", e=512),
                        W2F.rearrange("c p d -> p c d")[:, 8 * hb:8 * hb + 8,
                                                        grp * 512:(grp + 1) * 512],
                        "qG", waits=list(w))
                    if hb == 0:
                        for sp in range(4):
                            ds = grp * 4 + sp
                            w = [res_ev[(it - 1, sp)]] if it >= 1 else []
                            yload[(it, sp)] = p.dma("gpsimd", yt[sp][:],
                                                    YS[ds][:, half * 1024:(half + 1) * 1024],
                                                    "qG", waits=w)
                    w = [wload[(it, hb)], hload[(it, hb)]]
                    if hb == 0 and it >= 1:
                        w += [res_ev[(it - 1, s_)] for s_ in range(4)]

                    def blk(t, it=it, hb=hb):
                        gi = it * 8 + hb
                        for hh in range(8):
                            hk = hb * 8 + hh
                            for sp in range(4):
                                for tb in range(2):
                                    mm = t.matmul(
                                        out=psD[sp][:, tb * 512:(tb + 1) * 512],
                                        lhsT=w2fb[gi % 3][:, hh * 512 + sp * 128: hh * 512 + sp * 128 + 128],
                                        rhs=hid8[gi % 3][:, hh * 1024 + tb * 512: hh * 1024 + (tb + 1) * 512],
                                        start=(hk == 0), stop=(hk == NFH - 1))
                        return mm
                    blk_ev[(it, hb)] = p.step("tensor", blk, inc="sPE", waits=w)
                for sp in range(4):
                    ds = grp * 4 + sp
                    ob = ot[(it % 2) * 4 + sp]
                    w = [blk_ev[(it, 7)], yload[(it, sp)]]
                    if it >= 2:
                        w.append(oste[(it - 2, sp)])
                    res_ev[(it, sp)] = p.step("vector", (lambda v, sp=sp, ob=ob: v.tensor_tensor(
                        out=ob[:], in0=psD[sp][:], in1=yt[sp][:], op=ALU.add)),
                        inc="sDVE", waits=w)
                    oste[(it, sp)] = p.dma("gpsimd", out_fm[ds * 128:(ds + 1) * 128,
                                                           half * 1024:(half + 1) * 1024],
                                           ob[:], "qG", waits=[res_ev[(it, sp)]])
            p.step("gpsimd", lambda g: g.engine_nop(),
                   waits=p.queue_totals("qS") + p.queue_totals("qG"))

        block = es.enter_context(nc.Block())
        p.emit_all(block)
    build._last_prog = p
    return nc



def _hadamard(n):
    h = np.array([[1.0]], dtype=np.float32)
    while h.shape[0] < n:
        h = np.block([[h, h], [h, -h]])
    return h


_cc = {}


def _make_consts():
    if "c" not in _cc:
        import ml_dtypes
        c = np.zeros((128, 448), dtype=np.float32)
        k = np.arange(128)
        c[:, 0:128] = (k[:, None] <= k[None, :]).astype(np.float32)
        c[:, 128:256] = _hadamard(128)
        c[:, 256:384] = np.eye(128, dtype=np.float32)
        c[0:64, 384:448] = _hadamard(64)
        c[64:128, 384:448] = _hadamard(64)
        _cc["c"] = c.astype(ml_dtypes.bfloat16)
        kk = np.arange(128)
        _cc["mf"] = np.ascontiguousarray((kk[:, None] <= kk[None, :]).astype(np.float32))
    return _cc["c"], _cc["mf"]


def make_in_maps(L0_attn_x, L0_attn_Wq, L0_attn_bq, L0_attn_Wk, L0_attn_bk,
                 L0_attn_Wv, L0_attn_bv, L0_attn_Wo, L0_attn_bo,
                 L0_ffn_W1, L0_ffn_W2, L0_ffn_W3):
    x = np.asarray(L0_attn_x, dtype=np.float32)
    wq = np.ascontiguousarray(np.asarray(L0_attn_Wq, dtype=np.float32))
    wk = np.ascontiguousarray(np.asarray(L0_attn_Wk, dtype=np.float32))
    wv = np.ascontiguousarray(np.asarray(L0_attn_Wv, dtype=np.float32))
    wo = np.ascontiguousarray(np.asarray(L0_attn_Wo, dtype=np.float32))
    w1t = np.ascontiguousarray(np.asarray(L0_ffn_W1, dtype=np.float32).T)
    w3t = np.ascontiguousarray(np.asarray(L0_ffn_W3, dtype=np.float32).T)
    w2t = np.ascontiguousarray(np.asarray(L0_ffn_W2, dtype=np.float32).T)
    bq = np.asarray(L0_attn_bq, dtype=np.float32)
    bk = np.asarray(L0_attn_bk, dtype=np.float32)
    bv = np.asarray(L0_attn_bv, dtype=np.float32)
    bo = np.asarray(L0_attn_bo, dtype=np.float32)

    bcols = np.ascontiguousarray(np.concatenate(
        [bq.reshape(16, 128).T, bk.reshape(16, 128).T, bo.reshape(16, 128).T], axis=1))
    bvr = np.ascontiguousarray(bv[None, :])
    consts, maskf = _make_consts()
    epsc_v = np.full((1, 4), EPS_LN, dtype=np.float32)

    in_maps = []
    for c in range(8):
        b = c // 4
        s0 = (c % 4) * T
        xs = np.ascontiguousarray(x[b, s0:s0 + T, :].T)
        m8 = np.zeros((128, 8), dtype=np.float32)
        for r in range(8):
            if r // 4 == c // 4 and r < c:
                m8[:, r] = 1.0
        in_maps.append({
            "x_fm": xs, "wq": wq, "wk": wk, "wv": wv, "wo": wo,
            "w1t": w1t, "w3t": w3t, "w2t": w2t,
            "bcols": bcols, "bvrow": bvr, "consts": consts, "constf": maskf,
            "mask8": m8, "epsc": epsc_v,
        })
    return in_maps


def kernel(**inputs):
    in_maps = make_in_maps(**inputs)
    nc = build()
    res = run_bass_kernel_spmd(nc, in_maps, core_ids=list(range(8)))
    out = np.empty((2, 8192, D), dtype=np.float32)
    for c in range(8):
        b = c // 4
        s0 = (c % 4) * T
        out[b, s0:s0 + T, :] = res.results[c]["out_fm"].T
    return out
